# version 11
# baseline (speedup 1.0000x reference)
import numpy as np
import ml_dtypes
import concourse.bass as bass
import concourse.mybir as mybir
from concourse.bass_utils import run_bass_kernel_spmd
from contextlib import ExitStack

F32 = mybir.dt.float32
BF16 = mybir.dt.bfloat16
BF = ml_dtypes.bfloat16
AF = mybir.ActivationFunctionType
ALU = mybir.AluOpType
AX = mybir.AxisListType

B, L, D, H, HD = 8, 4096, 1024, 16, 64
TILE = 256
NT = L // TILE
NPAIR = 2
E, HH = 32, 256
BIG = 1e9
EPS = 1e-6
LNEPS = 1e-5



def _proj_img(w):
    return np.ascontiguousarray(
        w.T.reshape(8, 128, 8, 128).transpose(1, 0, 2, 3).reshape(128, 8192))


def _host_images(inputs):
    g = lambda n: np.asarray(inputs[n], np.float32)
    imgs = {}
    imgs["wq_img"] = _proj_img(g("L0_attn_Wq"))
    imgs["wk_img"] = _proj_img(g("L0_attn_Wk"))
    imgs["wv_img"] = _proj_img(g("L0_attn_Wv"))
    imgs["wo_img"] = _proj_img(g("L0_attn_Wo"))
    wb = g("L0_attn_Wbeta")
    imgs["wbeta_img"] = np.ascontiguousarray(
        wb.T.reshape(8, 128, 16).transpose(1, 0, 2).reshape(128, 128))
    rw = g("L0_ffn_router_w")
    imgs["router_img"] = np.ascontiguousarray(
        rw.T.reshape(8, 128, 32).transpose(1, 0, 2).reshape(128, 256))
    w1 = g("L0_ffn_weights1")
    imgs["w1_img"] = np.ascontiguousarray(
        w1.reshape(E, 8, 128, 2, 128).transpose(2, 0, 1, 3, 4).reshape(128, 65536))
    w2 = g("L0_ffn_weights2").reshape(E * HH, D)
    imgs["w2_img"] = np.ascontiguousarray(
        w2.reshape(64, 128, 8, 128).transpose(1, 2, 0, 3).reshape(128, 65536))
    bias = np.zeros((128, 32), np.float32)
    for j, n in enumerate(["L0_attn_bk", "L0_attn_bq", "L0_attn_bv", "L0_attn_bo"]):
        bias[:, j * 8:(j + 1) * 8] = g(n).reshape(8, 128).T
    imgs["bias_img"] = bias
    cw = np.zeros((128, 96), np.float32)
    for j, n in enumerate(["L0_attn_wk_conv", "L0_attn_wq_conv", "L0_attn_wv_conv"]):
        cw[:, j * 32:(j + 1) * 32] = g(n).reshape(8, 128, 4).transpose(1, 0, 2).reshape(128, 32)
    imgs["convw_img"] = cw
    cb = np.zeros((128, 24), np.float32)
    for j, n in enumerate(["L0_attn_bk_conv", "L0_attn_bq_conv", "L0_attn_bv_conv"]):
        cb[:, j * 8:(j + 1) * 8] = g(n).reshape(8, 128).T
    imgs["convb_img"] = cb
    imgs["b1_img"] = np.ascontiguousarray(
        g("L0_ffn_bias1").reshape(E, 2, 128).transpose(2, 0, 1).reshape(128, 64))
    imgs["b2_img"] = np.ascontiguousarray(g("L0_ffn_bias2"))
    imgs["rb_img"] = np.ascontiguousarray(g("L0_ffn_router_b").reshape(32, 1))
    imgs["bbeta_img"] = np.ascontiguousarray(g("L0_attn_bbeta").reshape(16, 1))
    imgs["nw_img"] = np.ascontiguousarray(np.tile(g("L0_attn_norm_w"), 2).reshape(128, 1))
    return imgs


def _host_consts():
    c = {}
    c["c_identb"] = np.eye(128, dtype=BF)
    c["c_identf"] = np.eye(32, dtype=np.float32)
    sl = np.zeros((128, 128), np.float32)
    for cc in range(2):
        sl[cc * 64:(cc + 1) * 64, cc * 64:(cc + 1) * 64] = np.tril(
            np.ones((64, 64), np.float32), -1)
    c["c_slneg"] = np.tile(-sl, (1, 8)).astype(BF)
    su = np.zeros((128, 128), np.float32)
    for cc in range(2):
        su[cc * 64:(cc + 1) * 64, cc * 64:(cc + 1) * 64] = np.triu(
            np.ones((64, 64), np.float32), 1)
    c["c_sun"] = np.tile(-su, (1, 8)).astype(BF)
    iu = np.zeros((128, 128), np.float32)
    for cc in range(2):
        iu[cc * 64:(cc + 1) * 64, cc * 64:(cc + 1) * 64] = np.triu(
            np.ones((64, 64), np.float32), 0)
    c["c_iu"] = iu.astype(BF)
    i64 = np.eye(64, dtype=np.float32)
    i2 = np.zeros((128, 128), np.float32)
    for a in (0, 64):
        for b_ in (0, 64):
            i2[a:a + 64, b_:b_ + 64] = i64
    c["c_i64"] = i2
    m16 = np.zeros((128, 128), np.float32)
    for cb in range(8):
        for p in range(128):
            m16[p, cb * 16 + 2 * cb + p // 64] = 1.0
    c["c_m16"] = m16.astype(BF)
    bm = np.zeros((16, 1024), np.float32)
    for cb in range(8):
        for p in range(128):
            bm[2 * cb + p // 64, cb * 128 + p] = 1.0
    c["c_bm16"] = bm.astype(BF)
    c["c_ones128"] = np.ones((128, 1), BF)
    c["c_bcast1"] = np.ones((1, 128), BF)
    c["c_bigc"] = (np.eye(32, dtype=np.float32) * BIG).astype(BF)
    epst = np.zeros((128, 4), np.float32)
    epst[:, 0] = EPS
    epst[:, 1] = LNEPS
    c["c_eps"] = epst
    return c



ENGS = ("tensor", "vector", "scalar", "gpsimd", "sync")


SYNC_EVERY = 3
PARANOID_T = False
AUTOSEM = {"vector": "aV", "scalar": "aS", "gpsimd": "aG"}


class Prog:
    def __init__(self):
        self.ops = {e: [] for e in ENGS}
        self.cnt = {}
        self.free = {}
        self.chain = {e: 0 for e in ENGS}

    def op(self, eng, build, inc=None):
        auto = None
        is_dma = inc is not None and inc.startswith("d")
        if eng in AUTOSEM and not is_dma:
            self.chain[eng] += 1
            if self.chain[eng] >= SYNC_EVERY:
                self.chain[eng] = 0
                if inc is None:
                    inc = AUTOSEM[eng]
                else:
                    auto = True
        if eng == "tensor" and PARANOID_T and inc is None:
            inc = "sT"
            auto = True
        if inc is not None:
            amt = 16 if inc.startswith("d") else 1
            if amt == 16:
                self.wait(eng, inc, self.cnt.get(inc, 0))
            self.cnt[inc] = self.cnt.get(inc, 0) + amt
            val = self.cnt[inc]

            def fn(e, sems, build=build, inc=inc, amt=amt):
                build(e).then_inc(sems[inc], amt)
            self.ops[eng].append(fn)
            if eng in AUTOSEM and (auto or inc == AUTOSEM[eng]):
                self.wait(eng, inc, val)
            if eng == "tensor" and PARANOID_T:
                self.wait(eng, inc, val)
            return val
        self.ops[eng].append(lambda e, sems, build=build: build(e))
        return None

    def wait(self, eng, name, val):
        if val is None or val <= 0:
            return
        self.ops[eng].append(
            lambda e, sems, name=name, val=val: e.wait_ge(sems[name], val))

    def drive(self, n=1):
        if getattr(self, "ffn_gen", None) is None:
            return
        for _ in range(n):
            if next(self.ffn_gen, "END") == "END":
                self.ffn_gen = None
                return
            self.gen_yields = getattr(self, "gen_yields", 0) + 1

    def drive_to(self, n):
        while (getattr(self, "ffn_gen", None) is not None
               and getattr(self, "gen_yields", 0) < n):
            self.drive(1)

    def drive_all(self):
        while getattr(self, "ffn_gen", None) is not None:
            self.drive(64)

    def wait_free(self, eng, res):
        if res in self.free:
            s, v = self.free[res]
            self.wait(eng, s, v)

    def set_free(self, res, sem, val):
        self.free[res] = (sem, val)



CONSTS = [("c_identb", [128, 128], BF16), ("c_identf", [32, 32], F32),
          ("c_slneg", [128, 1024], BF16), ("c_sun", [128, 1024], BF16), ("c_iu", [128, 128], BF16),
          ("c_i64", [128, 128], F32), ("c_m16", [128, 128], BF16),
          ("c_bm16", [16, 1024], BF16), ("c_ones128", [128, 1], BF16),
          ("c_bcast1", [1, 128], BF16), ("c_bigc", [32, 32], BF16),
          ("c_eps", [128, 4], F32)]


def build_nc():
    nc = bass.Bass("TRN2", target_bir_lowering=False, detect_race_conditions=False)
    dr = {}
    dr["x_img"] = nc.dram_tensor("x_img", [128, 32768], F32, kind="ExternalInput")
    for n in ["wq_img", "wk_img", "wv_img", "wo_img"]:
        dr[n] = nc.dram_tensor(n, [128, 8192], F32, kind="ExternalInput")
    dr["wbeta_img"] = nc.dram_tensor("wbeta_img", [128, 128], F32, kind="ExternalInput")
    dr["router_img"] = nc.dram_tensor("router_img", [128, 256], F32, kind="ExternalInput")
    dr["w1_img"] = nc.dram_tensor("w1_img", [128, 65536], F32, kind="ExternalInput")
    dr["w2_img"] = nc.dram_tensor("w2_img", [128, 65536], F32, kind="ExternalInput")
    for n, sh in [("bias_img", [128, 32]), ("convw_img", [128, 96]),
                  ("convb_img", [128, 24]), ("b1_img", [128, 64]),
                  ("b2_img", [32, 1024]), ("rb_img", [32, 1]),
                  ("bbeta_img", [16, 1]), ("nw_img", [128, 1])]:
        dr[n] = nc.dram_tensor(n, sh, F32, kind="ExternalInput")
    for n, sh, dt in CONSTS:
        dr[n] = nc.dram_tensor("d" + n, sh, dt, kind="ExternalInput")
    dr["out_img"] = nc.dram_tensor("out_img", [128, 32768], F32, kind="ExternalOutput")
    for n in ["wq_bf", "wk_bf", "wv_bf", "wo_bf"]:
        dr[n] = nc.dram_tensor(n, [128, 8192], BF16, kind="Internal")
    dr["w1_bf"] = nc.dram_tensor("w1_bf", [128, 65536], BF16, kind="Internal")
    dr["w2_bf"] = nc.dram_tensor("w2_bf", [128, 65536], BF16, kind="Internal")

    es = ExitStack()

    def sbt(name, shape, dtype):
        return es.enter_context(nc.sbuf_tensor(name, shape, dtype))

    t = {}
    for n, sh, dt in CONSTS:
        t[n] = sbt(n, sh, dt)
    t["w_beta"] = sbt("w_beta", [128, 128], BF16)
    t["w_router"] = sbt("w_router", [128, 256], BF16)
    t["w_b2"] = sbt("w_b2", [32, 1024], BF16)
    t["w_nw"] = sbt("w_nw", [128, 1], F32)
    t["w_bias"] = sbt("w_bias", [128, 32], F32)
    t["w_cw"] = sbt("w_cw", [128, 96], F32)
    t["w_cb"] = sbt("w_cb", [128, 24], F32)
    t["w_b1"] = sbt("w_b1", [128, 64], F32)
    t["w_rb"] = sbt("w_rb", [32, 1], F32)
    t["w_bbeta"] = sbt("w_bbeta", [16, 1], F32)
    t["wpA"] = sbt("wpA", [128, 8192], BF16)
    t["wpB"] = sbt("wpB", [128, 8192], BF16)
    t["wf0"] = sbt("wf0", [128, 4096], BF16)
    t["wf1"] = sbt("wf1", [128, 4096], BF16)
    t["wf2"] = sbt("wf2", [128, 4096], BF16)
    t["H_cat"] = sbt("H_cat", [128, 16384], BF16)
    t["x_bf"] = sbt("x_bf", [128, 2048], BF16)
    for n in ["kp", "qp", "vp"]:
        t[n] = sbt(n, [128, 8 * 259], BF16)
    for n in ["kc", "qc", "k_n", "q_n", "v_n", "kb_a", "vb_a"]:
        t[n] = sbt(n, [128, 2048], BF16)
    t["beta_b"] = sbt("beta_b", [16, 256], BF16)
    t["T_st"] = sbt("T_st", [128, 1024], F32)
    t["U_bd"] = sbt("U_bd", [128, 2048], F32)
    t["Y0"] = sbt("Y0", [128, 2048], F32)
    t["Ybf"] = sbt("Ybf", [128, 2048], BF16)
    t["MT"] = sbt("MT", [128, 2048], BF16)
    t["Kst"] = sbt("Kst", [128, 1024], BF16)
    t["Qt"] = sbt("Qt", [128, 1024], BF16)
    t["Ot"] = sbt("Ot", [128, 1024], BF16)
    t["PT"] = sbt("PT", [128, 1024], BF16)
    t["Rr"] = sbt("Rr", [128, 1024], BF16)
    t["S_bf"] = sbt("S_bf", [128, 1024], BF16)
    t["O_tile"] = t["qc"]
    t["onorm"] = t["kc"]
    t["s_bf"] = sbt("s_bf", [128, 2048], BF16)
    t["vc"] = t["vb_a"]
    t["y_bf"] = sbt("y_bf", [128, 2048], BF16)
    t["s2_bf"] = sbt("s2_bf", [128, 2048], BF16)
    t["out_f"] = sbt("out_f", [128, 1024], F32)
    t["logits"] = sbt("logits", [32, 256], F32)
    t["lgTd"] = sbt("lgTd", [128, 64], F32)
    t["lgmax"] = sbt("lgmax", [128, 2], F32)
    t["maskTd"] = sbt("maskTd", [128, 64], BF16)
    t["mask_b"] = sbt("mask_b", [32, 256], BF16)
    t["premask"] = sbt("premask", [32, 256], BF16)
    t["nstat"] = sbt("nstat", [16, 768], F32)
    t["fkb"] = sbt("fkb", [16, 256], BF16)
    t["fqb"] = sbt("fqb", [16, 256], BF16)
    t["frb"] = sbt("frb", [16, 256], BF16)

    ps = {}
    ps["pmA"] = es.enter_context(nc.psum_tensor("pmA", [128, 512], F32))
    ps["pmB"] = es.enter_context(nc.psum_tensor("pmB", [128, 512], F32))
    ps["psA"] = es.enter_context(nc.psum_tensor("psA", [128, 1024], F32))
    ps["psB"] = es.enter_context(nc.psum_tensor("psB", [128, 1024], F32))
    ps["pTR"] = es.enter_context(nc.psum_tensor("pTR", [128, 1024], BF16))
    ps["pMS"] = es.enter_context(nc.psum_tensor("pMS", [128, 512], F32))

    sems = {}
    for n in ["sT", "sV", "sS", "sG", "sY", "aV", "aS", "aG", "dY", "dS", "dG",
              "dWPA", "dWPB", "dW1A", "dW1B", "dW2A", "dW2B"]:
        sems[n] = es.enter_context(nc.semaphore(n))

    P = Prog()
    st = {"wp_load": {}, "w1_load": {}, "w2_load": {}, "x_cast": {}}
    emit_init(P, t, ps, dr, st)
    P.ffn_gen = None
    for ti in range(NT):
        emit_p1(P, t, ps, dr, st, ti)
        emit_p2(P, t, ps, dr, st, ti)
        emit_p3(P, t, ps, dr, st, ti)
        for pr in range(NPAIR):
            emit_solve_pair(P, t, ps, st, ti, pr)
            for half in range(2):
                P.drive(1)
                emit_seq_chunk(P, t, ps, st, ti, pr * 2 + half)
        st[("kqv_lastuse", ti)] = st[("solve_lastkqv", ti, 1)]
        emit_rms_wo_ln1(P, t, ps, dr, st, ti)
        P.drive_all()
        P.ffn_gen = ffn_chunks(P, t, ps, dr, st, ti)
        P.gen_yields = 0
    P.drive_all()
    P.wait("sync", "dS", P.cnt.get("dS", 0))

    with nc.Block() as block:
        @block.tensor
        def _(e):
            for fn in P.ops["tensor"]:
                fn(e, sems)

        @block.vector
        def _(e):
            for fn in P.ops["vector"]:
                fn(e, sems)

        @block.scalar
        def _(e):
            for fn in P.ops["scalar"]:
                fn(e, sems)

        @block.gpsimd
        def _(e):
            for fn in P.ops["gpsimd"]:
                fn(e, sems)

        @block.sync
        def _(e):
            for fn in P.ops["sync"]:
                fn(e, sems)

    es.close()
    return nc


def emit_init(P, t, ps, dr, st):
    op, wait = P.op, P.wait
    for n, _, _ in CONSTS:
        op("sync", lambda e, n=n: e.dma_start(t[n][:, :], dr[n][:, :]), inc="dY")
    for src, dst in [("bias_img", "w_bias"), ("convw_img", "w_cw"),
                     ("convb_img", "w_cb"), ("b1_img", "w_b1"),
                     ("rb_img", "w_rb"), ("bbeta_img", "w_bbeta"),
                     ("nw_img", "w_nw")]:
        op("sync", lambda e, s=src, d=dst: e.dma_start(t[d][:, :], dr[s][:, :]), inc="dY")
    wait("sync", "dY", P.cnt["dY"])
    st["init_Y"] = op("sync", lambda e: e.nop(), inc="sY")
    op("gpsimd", lambda e: e.dma_start(t["w_beta"][:, :], dr["wbeta_img"][:, :]), inc="dG")
    op("gpsimd", lambda e: e.dma_start(t["w_router"][:, :], dr["router_img"][:, :]), inc="dG")
    op("gpsimd", lambda e: e.dma_start(t["w_b2"][:, :], dr["b2_img"][:, :]), inc="dG")
    wait("gpsimd", "dG", P.cnt["dG"])

    op("gpsimd", lambda e: e.memset(t["S_bf"][:, :], 0.0))
    op("gpsimd", lambda e: e.memset(t["U_bd"][:, :], 0.0))
    for n in ["kp", "qp", "vp"]:
        op("gpsimd", lambda e, n=n: e.memset(t[n][:, :], 0.0))
    st["init_G"] = op("gpsimd", lambda e: e.nop(), inc="sG")

    for img, scr in [("wk_img", "wk_bf"), ("wq_img", "wq_bf"), ("wv_img", "wv_bf")]:
        v = op("gpsimd", lambda e, img=img: e.dma_start(t["wpA"][:, :], dr[img][:, :]),
               inc="dG")
        wait("sync", "dG", v)
        v2 = op("sync", lambda e, scr=scr: e.dma_start(dr[scr][:, :], t["wpA"][:, :]),
                inc="dY")
        wait("gpsimd", "dY", v2)
    v = op("gpsimd", lambda e: e.dma_start(t["wpA"][:, :], dr["wo_img"][:, :]), inc="dG")
    wait("vector", "dG", v)
    wait("vector", "sY", st["init_Y"])
    v = op("vector", lambda e: e.tensor_single_scalar(
        t["wpB"][:, :], t["wpA"][:, :], t["w_nw"][:, 0:1], ALU.mult), inc="sV")
    wait("sync", "sV", v)
    op("sync", lambda e: e.dma_start(dr["wo_bf"][:, :], t["wpB"][:, :]), inc="dY")

    prev_store = None
    for img, scr in [("w1_img", "w1_bf"), ("w2_img", "w2_bf")]:
        for ch in range(8):
            half = (ch % 2) * 8192
            if prev_store is not None:
                wait("gpsimd", "dY", prev_store)
            v = op("gpsimd", lambda e, img=img, ch=ch, half=half: e.dma_start(
                t["H_cat"][:, half:half + 8192],
                dr[img][:, ch * 8192:(ch + 1) * 8192]), inc="dG")
            wait("sync", "dG", v)
            prev_store = op("sync", lambda e, scr=scr, ch=ch, half=half: e.dma_start(
                dr[scr][:, ch * 8192:(ch + 1) * 8192],
                t["H_cat"][:, half:half + 8192]), inc="dY")

    st["wp_load"][(0, 0)] = ("dWPA", op(
        "sync", lambda e: e.dma_start(t["wpA"][:, :], dr["wk_bf"][:, :]), inc="dWPA"))
    st["wp_load"][(0, 1)] = ("dWPB", op(
        "sync", lambda e: e.dma_start(t["wpB"][:, :], dr["wq_bf"][:, :]), inc="dWPB"))
    st["x_cast"][0] = op("gpsimd", lambda e: e.dma_start(
        t["x_bf"][:, :], dr["x_img"][:, 0:2048]), inc="dG")
    st[("S_ready", 0)] = None


def _engs(P, ps):
    return (ps[n] for n in ["pmA", "pmB", "psA", "psB", "pTR", "pMS"])


def emit_p1(P, t, ps, dr, st, ti):
    op, wait = P.op, P.wait
    pmA, pmB, psA, psB, pTR, pMS = _engs(P, ps)

    wait("tensor", "dG", st["x_cast"][ti])
    if ti == 0:
        wait("tensor", "sY", st["init_Y"])
        wait("tensor", "sG", st["init_G"])
        wait("tensor", "sG", st["init_G"])
    projs = [("kp", "wpA", 0), ("qp", "wpB", 8), ("vp", "wpA", 16)]
    for pi, (arena, wbuf, bcol) in enumerate(projs):
        wait("tensor", *st["wp_load"][(ti, pi)])
        if ti > 0 and pi == 0:
            wait("tensor", "sV", st[("conv_done", ti - 1)])
        if ti > 0 and pi == 2:
            wait("tensor", "sV", st[("conv_done", ti - 1)])
        for mb in range(8):
            half = (mb % 2) * 512
            pn = "psA0" if mb % 2 == 0 else "psA1"
            P.wait_free("tensor", pn)
            for kb in range(8):
                v = op("tensor", lambda e, half=half, wbuf=wbuf, kb=kb, mb=mb: e.matmul(
                    psA[:, half:half + 256],
                    t[wbuf][:, kb * 1024 + mb * 128:kb * 1024 + mb * 128 + 128],
                    t["x_bf"][:, kb * 256:(kb + 1) * 256],
                    start=(kb == 0), stop=(kb == 7)),
                    inc=("sT" if kb == 7 else None))
            wait("scalar", "sT", v)
            if ti == 0 and pi == 0 and mb == 0:
                wait("scalar", "sY", st["init_Y"])
            if ti > 0 and mb == 0 and pi == 0:
                wait("scalar", "sV", st[("conv_done", ti - 1)])
            if ti > 0 and mb == 0 and pi == 2:
                wait("scalar", "sV", st[("conv_done", ti - 1)])
            ev = op("scalar", lambda e, half=half, arena=arena, mb=mb, bcol=bcol:
                    e.activation(t[arena][:, mb * 259 + 3:mb * 259 + 259],
                                 psA[:, half:half + 256], AF.Identity,
                                 bias=t["w_bias"][:, bcol + mb:bcol + mb + 1]),
                    inc="sS")
            P.set_free(pn, "sS", ev)
        st[("proj_done", ti, pi)] = ev
        st[("projmm", ti, pi)] = v
        wait("sync", "sT", v)
        if pi == 0:
            st["wp_load"][(ti, 2)] = ("dWPA", op(
                "sync", lambda e: e.dma_start(t["wpA"][:, :], dr["wv_bf"][:, :]),
                inc="dWPA"))
        elif pi == 1:
            st["wp_load"][(ti, 3)] = ("dWPB", op(
                "sync", lambda e: e.dma_start(t["wpB"][:, :], dr["wo_bf"][:, :]),
                inc="dWPB"))
        elif ti + 1 < NT:
            st["wp_load"][(ti + 1, 0)] = ("dWPA", op(
                "sync", lambda e: e.dma_start(t["wpA"][:, :], dr["wk_bf"][:, :]),
                inc="dWPA"))
        P.drive(1)
    st[("proj_mm_done", ti)] = P.cnt["sT"]

    P.wait_free("tensor", "pMS")
    if ti > 0:
        wait("scalar", "sT", st[("beta_lastuse", ti - 1)])
    for kb in range(8):
        v = op("tensor", lambda e, kb=kb: e.matmul(
            pMS[0:16, 0:256], t["w_beta"][:, kb * 16:(kb + 1) * 16],
            t["x_bf"][:, kb * 256:(kb + 1) * 256],
            start=(kb == 0), stop=(kb == 7)), inc=("sT" if kb == 7 else None))
    wait("scalar", "sT", v)
    ev = op("scalar", lambda e: e.activation(t["beta_b"][:, :], pMS[0:16, 0:256],
                                             AF.Sigmoid, bias=t["w_bbeta"][:, 0:1]),
            inc="sS")
    st[("beta", ti)] = ev
    P.set_free("pMS", "sS", ev)


def emit_p2(P, t, ps, dr, st, ti):
    op, wait = P.op, P.wait
    pmA, pmB, psA, psB, pTR, pMS = _engs(P, ps)
    for pi, (arena, cout, eng) in enumerate([("kp", "kc", "vector"),
                                             ("qp", "qc", "vector"),
                                             ("vp", "vc", "vector")]):
        wait(eng, "sS", st[("proj_done", ti, pi)])
        if ti == 0:
            wait(eng, "sY", st["init_Y"])
            wait(eng, "sG", st["init_G"])
        if ti > 0:
            wait(eng, "sS", st[("sq_done", ti - 1)])
            wait(eng, "sV", st[("p3mul_done", ti - 1)])
        sem = "sV" if eng == "vector" else "sG"
        for cb in range(8):
            base = cb * 259
            wc = pi * 32 + cb * 4
            op(eng, lambda e, arena=arena, cout=cout, base=base, wc=wc, cb=cb, pi=pi:
               e.tensor_scalar(t[cout][:, cb * 256:(cb + 1) * 256],
                               t[arena][:, base + 0:base + 256],
                               t["w_cw"][:, wc:wc + 1],
                               t["w_cb"][:, pi * 8 + cb:pi * 8 + cb + 1],
                               ALU.mult, ALU.add))
            for tap in (1, 2, 3):
                op(eng, lambda e, arena=arena, cout=cout, base=base, wc=wc, cb=cb, tap=tap:
                   e.scalar_tensor_tensor(t[cout][:, cb * 256:(cb + 1) * 256],
                                          t[arena][:, base + tap:base + tap + 256],
                                          t["w_cw"][:, wc + tap:wc + tap + 1],
                                          t[cout][:, cb * 256:(cb + 1) * 256],
                                          ALU.mult, ALU.add))
        ev = op(eng, lambda e, arena=arena: e.tensor_copy(
            t[arena][:, :].rearrange("p (cb a) -> p cb a", a=259)[:, :, 0:3],
            t[arena][:, :].rearrange("p (cb a) -> p cb a", a=259)[:, :, 256:259]),
            inc=sem)
        st[("conv", ti, pi)] = (sem, ev)
        P.drive(2)
    st[("conv_done", ti)] = P.cnt["sV"]
    st[("conv_done_g", ti)] = P.cnt["sG"]


def emit_p3(P, t, ps, dr, st, ti):
    op, wait = P.op, P.wait
    pmA, pmB, psA, psB, pTR, pMS = _engs(P, ps)
    if ti > 0:
        wait("vector", "sT", st[("kqv_lastuse", ti - 1)])
        wait("scalar", "sT", st[("ln1stats", ti - 1)])
    for src, col0, fb in [("kc", 0, "fkb"), ("qc", 256, "fqb")]:
        wait("scalar", "sV", st[("conv_done", ti)])
        if src == "qc":
            wait("scalar", "sT", st[(ti, "kred")])
        sqv = op("scalar", lambda e, src=src: e.activation(
            t["kb_a"][:, :], t[src][:, :], AF.Square), inc="sS")
        wait("tensor", "sS", sqv)
        P.wait_free("tensor", "pMS")
        for cb in range(8):
            v = op("tensor", lambda e, cb=cb: e.matmul(
                pMS[0:16, 0:256], t["c_m16"][:, cb * 16:(cb + 1) * 16],
                t["kb_a"][:, cb * 256:(cb + 1) * 256],
                start=(cb == 0), stop=(cb == 7)), inc=("sT" if cb == 7 else None))
        st[(ti, "kred")] = v
        wait("scalar", "sT", v)
        sq2 = op("scalar", lambda e, col0=col0: e.activation(
            t["nstat"][:, col0:col0 + 256], pMS[0:16, 0:256], AF.Sqrt), inc="sS")
        P.set_free("pMS", "sS", sq2)
        wait("vector", "sS", sq2)
        op("vector", lambda e, col0=col0: e.tensor_single_scalar(
            t["nstat"][:, 512:768],
            t["nstat"][:, col0:col0 + 256], EPS, ALU.add))
        op("vector", lambda e, col0=col0: e.reciprocal(
            t["nstat"][:, col0:col0 + 256], t["nstat"][:, 512:768]))
        st[(ti, "f", src)] = op("vector", lambda e, fb=fb, col0=col0: e.tensor_copy(
            t[fb][:, :], t["nstat"][:, col0:col0 + 256]), inc="sV")
    wait("scalar", "sV", st[("conv_done", ti)])
    wait("scalar", "sT", st[(ti, "kred")])
    vsg = op("scalar", lambda e: e.activation(t["kb_a"][:, :], t["vc"][:, :],
                                              AF.Sigmoid), inc="sS")
    st[("sq_done", ti)] = vsg
    P.drive(1)

    lastmul = None
    for cb in range(8):
        P.drive(1)
        wait("tensor", "sV", st[(ti, "f", "kc")])
        if lastmul is not None:
            wait("tensor", "sV", lastmul)
        P.wait_free("tensor", "pMS")
        vK = op("tensor", lambda e, cb=cb: e.matmul(
            pMS[:, 256:512], t["c_bm16"][:, cb * 128:(cb + 1) * 128],
            t["fkb"][:, :], start=True, stop=True), inc="sT")
        wait("vector", "sT", vK)
        op("vector", lambda e, cb=cb: e.tensor_mul(
            t["k_n"][:, cb * 256:(cb + 1) * 256], t["kc"][:, cb * 256:(cb + 1) * 256],
            pMS[:, 256:512]))
        wait("vector", "sS", vsg)
        op("vector", lambda e, cb=cb: e.tensor_mul(
            t["v_n"][:, cb * 256:(cb + 1) * 256], t["vc"][:, cb * 256:(cb + 1) * 256],
            t["kb_a"][:, cb * 256:(cb + 1) * 256]))
        vmul = op("vector", lambda e, cb=cb: e.tensor_mul(
            t["v_n"][:, cb * 256:(cb + 1) * 256], t["v_n"][:, cb * 256:(cb + 1) * 256],
            pMS[:, 256:512]), inc="sV")
        wait("tensor", "sV", st[(ti, "f", "qc")])
        wait("tensor", "sV", vmul)
        vQ = op("tensor", lambda e, cb=cb: e.matmul(
            pMS[:, 256:512], t["c_bm16"][:, cb * 128:(cb + 1) * 128],
            t["fqb"][:, :], start=True, stop=True), inc="sT")
        wait("vector", "sT", vQ)
        vmul = op("vector", lambda e, cb=cb: e.tensor_mul(
            t["q_n"][:, cb * 256:(cb + 1) * 256], t["qc"][:, cb * 256:(cb + 1) * 256],
            pMS[:, 256:512]), inc="sV")
        wait("tensor", "sS", st[("beta", ti)])
        wait("tensor", "sV", vmul)
        vB = op("tensor", lambda e, cb=cb: e.matmul(
            pMS[:, 256:512], t["c_bm16"][:, cb * 128:(cb + 1) * 128],
            t["beta_b"][:, :], start=True, stop=True), inc="sT")
        wait("vector", "sT", vB)
        op("vector", lambda e, cb=cb: e.tensor_mul(
            t["kb_a"][:, cb * 256:(cb + 1) * 256], t["k_n"][:, cb * 256:(cb + 1) * 256],
            pMS[:, 256:512]))
        lastmul = op("vector", lambda e, cb=cb: e.tensor_mul(
            t["vb_a"][:, cb * 256:(cb + 1) * 256], t["v_n"][:, cb * 256:(cb + 1) * 256],
            pMS[:, 256:512]), inc="sV")
    st[("qkv_done", ti)] = P.cnt["sV"]
    st[("p3mul_done", ti)] = P.cnt["sV"]
    st[("beta_lastuse", ti)] = P.cnt["sT"]
    P.set_free("pMS", "sV", lastmul)


def emit_solve_pair(P, t, ps, st, ti, pr):
    op, wait = P.op, P.wait
    psA, psB, pTR = ps["psA"], ps["psB"], ps["pTR"]
    po = pr * 128

    def sl(arena, h, w0, wn):
        par, hp = h % 2, h // 2
        return t[arena][par * 64:par * 64 + 64,
                        hp * 256 + po + w0:hp * 256 + po + w0 + wn]

    def solve16(dst_maker, lhs_maker, rhs_maker, tag):
        order = [h for h in range(16) if h % 2 == 0] + \
                [h for h in range(16) if h % 2 == 1]
        vA = vB = None
        for i, h in enumerate(order):
            pp = psA if h < 8 else psB
            last_even = (i == 7)
            v = op("tensor",
                   lambda e, h=h, pp=pp: e.matmul(
                       dst_maker(pp, h), lhs_maker(h), rhs_maker(h),
                       start=True, stop=True,
                       tile_position=(64 * (h % 2), 0)),
                   inc=("sT" if (last_even or i == 15) else None))
            if last_even:
                wait("tensor", "sT", v)
            if i == 15:
                vA = vB = v
        return vA, vB

    wait("tensor", "sV", st[("qkv_done", ti)])
    P.wait_free("tensor", "psA0")
    P.wait_free("tensor", "psA1")
    P.wait_free("tensor", "psB")

    vA, vB = solve16(lambda pp, h: pp[:, (h % 8) * 128:(h % 8) * 128 + 128],
                     lambda h: sl("kb_a", h, 0, 128),
                     lambda h: sl("k_n", h, 0, 128), "G")
    wait("vector", "sT", vB)
    for (pp, hs) in [(psA, 0), (psB, 8)]:
        for part, c0 in [(0, 0), (64, 64)]:
            vT0 = op("vector", lambda e, pp=pp, hs=hs, part=part, c0=c0: e.tensor_tensor(
                t["T_st"][part:part + 64, :].rearrange("p (h a) -> p h a", a=64)
                [:, hs:hs + 8, :],
                pp[part:part + 64, :].rearrange("p (h a) -> p h a", a=128)[:, :, c0:c0 + 64],
                t["c_slneg"][part:part + 64, :].rearrange("p (h a) -> p h a", a=128)
                [:, 0:8, c0:c0 + 64],
                ALU.mult), inc=("sV" if (hs == 8 and part == 64) else None))

    wait("tensor", "sV", vT0)
    vA, vB = solve16(lambda pp, h: pp[:, (h % 8) * 128:(h % 8) * 128 + 128],
                     lambda h: sl("k_n", h, 0, 128),
                     lambda h: sl("kb_a", h, 0, 128), "U0")
    wait("vector", "sT", vB)
    op("vector", lambda e: e.tensor_tensor(
        t["U_bd"][0:64, :].rearrange("p (h a) -> p h a", a=128)[:, 0:8, 0:64],
        psA[0:64, :].rearrange("p (h a) -> p h a", a=128)[:, :, 0:64],
        t["c_slneg"][0:64, :].rearrange("p (h a) -> p h a", a=128)[:, 0:8, 0:64] if False else
        t["c_sun"][0:64, :].rearrange("p (h a) -> p h a", a=128)[:, 0:8, 0:64],
        ALU.mult))
    op("vector", lambda e: e.tensor_tensor(
        t["U_bd"][64:128, :].rearrange("p (h a) -> p h a", a=128)[:, 0:8, 64:128],
        psA[64:128, :].rearrange("p (h a) -> p h a", a=128)[:, :, 64:128],
        t["c_sun"][64:128, :].rearrange("p (h a) -> p h a", a=128)[:, 0:8, 64:128],
        ALU.mult))
    op("vector", lambda e: e.tensor_tensor(
        t["U_bd"][0:64, :].rearrange("p (h a) -> p h a", a=128)[:, 8:16, 0:64],
        psB[0:64, :].rearrange("p (h a) -> p h a", a=128)[:, :, 0:64],
        t["c_sun"][0:64, :].rearrange("p (h a) -> p h a", a=128)[:, 0:8, 0:64],
        ALU.mult))
    vU0 = op("vector", lambda e: e.tensor_tensor(
        t["U_bd"][64:128, :].rearrange("p (h a) -> p h a", a=128)[:, 8:16, 64:128],
        psB[64:128, :].rearrange("p (h a) -> p h a", a=128)[:, :, 64:128],
        t["c_sun"][64:128, :].rearrange("p (h a) -> p h a", a=128)[:, 0:8, 64:128],
        ALU.mult), inc="sV")

    wait("tensor", "sV", vU0)
    vA, vB = solve16(lambda pp, h: pp[:, (h % 8) * 128:(h % 8) * 128 + 128],
                     lambda h: sl("k_n", h, 0, 128),
                     lambda h: sl("q_n", h, 0, 128), "MT")
    wait("vector", "sT", vA)
    op("vector", lambda e: e.tensor_tensor(
        t["MT"][:, 0:1024].rearrange("p (h a) -> p h a", a=128),
        psA[:, :].rearrange("p (h a) -> p h a", a=128),
        bass.AP(t["c_iu"], 0, [[128, 128], [0, 8], [1, 128]]), ALU.mult))
    wait("vector", "sT", vB)
    vMT = op("vector", lambda e: e.tensor_tensor(
        t["MT"][:, 1024:2048].rearrange("p (h a) -> p h a", a=128),
        psB[:, :].rearrange("p (h a) -> p h a", a=128),
        bass.AP(t["c_iu"], 0, [[128, 128], [0, 8], [1, 128]]), ALU.mult), inc="sV")
    st[("MT_done", ti, pr)] = vMT
    P.drive(1)

    if (ti, pr) != (0, 0):
        P.wait_free("tensor", "pTR")
    prev_ev = None
    for gi, (src, dst, dof) in enumerate([("k_n", "Kst", None),
                                          ("kb_a", "Y0", 0), ("vb_a", "Y0", 64)]):
        if prev_ev is not None:
            wait("tensor", "sS", prev_ev)
        order = [h for h in range(16) if h % 2 == 0] + \
                [h for h in range(16) if h % 2 == 1]
        for i, h in enumerate(order):
            pb = (h % 2) * 64
            v = op("tensor", lambda e, src=src, h=h, pb=pb: e.transpose(
                pTR[:, h * 64:(h + 1) * 64], sl(src, h, 0, 128),
                t["c_identb"][pb:pb + 64, pb:pb + 64],
                tile_position=(pb, 0)),
                inc=("sT" if i in (7, 15) else None))
            if i == 7:
                wait("tensor", "sT", v)
        wait("scalar", "sT", v)
        if dof is None:
            prev_ev = op("scalar", lambda e: e.copy(
                t["Kst"][:, :], pTR[:, :]), inc="sS")
        else:
            prev_ev = op("scalar", lambda e, dof=dof: e.copy(
                t["Y0"][:, :].rearrange("p (h a) -> p h a", a=128)[:, :, dof:dof + 64],
                pTR[:, :].rearrange("p (h a) -> p h a", a=64)[:, :, :]), inc="sS")
    vY0 = prev_ev

    P.set_free("pTR", "sS", vY0)
    vU = ("sS", vY0)

    for mlev in range(6):
        wait("tensor", *vU)
        if mlev == 0:
            wait("tensor", "sV", vU0)
        vA = vB = None
        for h in range(16):
            pp = psA if h < 8 else psB
            hc = (h % 8) * 128
            v = op("tensor", lambda e, h=h, pp=pp, hc=hc: e.matmul(
                pp[:, hc:hc + 128], t["U_bd"][:, h * 128:(h + 1) * 128],
                t["Y0"][:, h * 128:(h + 1) * 128], start=True, stop=True),
                inc=("sT" if h in (7, 15) else None))
            if h == 7:
                vA = v
            else:
                vB = v
        wait("vector", "sT", vA)
        op("vector", lambda e: e.tensor_add(t["Y0"][:, 0:1024], t["Y0"][:, 0:1024],
                                            psA[:, :]))
        wait("vector", "sT", vB)
        vY = op("vector", lambda e: e.tensor_add(t["Y0"][:, 1024:2048],
                                                 t["Y0"][:, 1024:2048], psB[:, :]),
                inc="sV")
        P.drive(1)
        if mlev == 5:
            wait("scalar", "sV", vY)
            op("scalar", lambda e: e.copy(t["Ybf"][:, 0:1024], t["Y0"][:, 0:1024]))
            vYb = op("scalar", lambda e: e.copy(t["Ybf"][:, 1024:2048],
                                                t["Y0"][:, 1024:2048]), inc="sS")
            st[("Y_done", ti, pr)] = vY
            st[("Ybf_done", ti, pr)] = vYb
            break
        wait("tensor", "sV", vY)
        for h in range(16):
            v = op("tensor", lambda e, h=h: e.matmul(
                psA[:, h * 64:(h + 1) * 64], t["U_bd"][:, h * 128:(h + 1) * 128],
                t["T_st"][:, h * 64:(h + 1) * 64], start=True, stop=True),
                inc=("sT" if h == 15 else None))
        vTn = v
        for h in range(16):
            for cc in range(2):
                v = op("tensor", lambda e, h=h, cc=cc: e.matmul(
                    psB[cc * 64:cc * 64 + 64, h * 64:(h + 1) * 64],
                    t["T_st"][cc * 64:cc * 64 + 64, h * 64:(h + 1) * 64],
                    t["U_bd"][cc * 64:cc * 64 + 64,
                              h * 128 + cc * 64:h * 128 + cc * 64 + 64],
                    start=True, stop=True,
                    tile_position=(cc * 64, cc * 64)),
                    inc=("sT" if (h, cc) == (15, 1) else None))
        vUn = v
        wait("scalar", "sT", vTn)
        wait("scalar", "sT", vUn)
        op("scalar", lambda e: e.copy(t["T_st"][:, :], psA[:, :]))
        op("scalar", lambda e: e.copy(
            t["U_bd"][0:64, :].rearrange("p (h a) -> p h a", a=128)[:, :, 0:64],
            psB[0:64, :].rearrange("p (h a) -> p h a", a=64)[:, :, :]))
        vU = ("sS", op("scalar", lambda e: e.copy(
            t["U_bd"][64:128, :].rearrange("p (h a) -> p h a", a=128)[:, :, 64:128],
            psB[64:128, :].rearrange("p (h a) -> p h a", a=64)[:, :, :]), inc="sS"))

    wait("tensor", "sV", st[("Y_done", ti, pr)])
    wait("tensor", "sS", st[("Ybf_done", ti, pr)])
    if ti > 0 or pr > 0:
        key = ("seq_mm_done", ti - 1, 3) if pr == 0 else ("seq_mm_done", ti, 1)
        if key in st:
            pass
    vA = vB = None
    for h in range(16):
        par, hp = h % 2, h // 2
        v = op("tensor", lambda e, h=h, par=par, hp=hp: e.matmul(
            psA[par * 64:par * 64 + 64, hp * 128:hp * 128 + 128],
            t["Ybf"][:, h * 128:h * 128 + 64],
            t["MT"][:, h * 128:(h + 1) * 128],
            start=True, stop=True, tile_position=(0, 64 * par)),
            inc=("sT" if h == 15 else None))
    vMW = v
    for h in range(16):
        par, hp = h % 2, h // 2
        v = op("tensor", lambda e, h=h, par=par, hp=hp: e.matmul(
            psB[par * 64:par * 64 + 64, hp * 128:hp * 128 + 128],
            t["Ybf"][:, h * 128 + 64:h * 128 + 128],
            t["MT"][:, h * 128:(h + 1) * 128],
            start=True, stop=True, tile_position=(0, 64 * par)),
            inc=("sT" if h == 15 else None))
    vMU = v
    prevseq = st.get(("seq_mm_done", ti, 1)) if pr == 1 else (
        st.get(("seq_mm_done", ti - 1, 3)) if ti > 0 else None)
    wait("vector", "sT", vMW)
    if prevseq is not None:
        wait("vector", "sT", prevseq)
        wait("scalar", "sT", prevseq)
    vQt = op("vector", lambda e: e.tensor_sub(
        t["Qt"][:, :],
        t["q_n"][:, :].rearrange("p (hp a) -> p hp a", a=256)[:, :, po:po + 128],
        psA[:, :].rearrange("p (hp a) -> p hp a", a=128)), inc="sV")
    wait("scalar", "sT", vMU)
    vOt = op("scalar", lambda e: e.copy(
        t["Ot"][:, :], psB[:, :]), inc="sS")
    wait("tensor", "sV", vQt)
    for cc in range(2):
        for h in range(16):
            par, hp = h % 2, h // 2
            v = op("tensor", lambda e, h=h, par=par, hp=hp, cc=cc: e.matmul(
                psA[par * 64:par * 64 + 64,
                    hp * 128 + cc * 64:hp * 128 + cc * 64 + 64],
                t["Ybf"][cc * 64:cc * 64 + 64, h * 128:h * 128 + 64],
                t["Kst"][cc * 64:cc * 64 + 64, h * 64:(h + 1) * 64],
                start=True, stop=True, tile_position=(64 * cc, 64 * par)),
                inc=("sT" if h == 15 else None))
        if cc == 0:
            wait("tensor", "sT", v)
    vPT = v
    wait("tensor", "sS", vOt)
    for cc in range(2):
        for h in range(16):
            par, hp = h % 2, h // 2
            v = op("tensor", lambda e, h=h, par=par, hp=hp, cc=cc: e.matmul(
                psB[par * 64:par * 64 + 64,
                    hp * 128 + cc * 64:hp * 128 + cc * 64 + 64],
                t["Kst"][cc * 64:cc * 64 + 64, h * 64:(h + 1) * 64],
                t["Ybf"][cc * 64:cc * 64 + 64, h * 128 + 64:h * 128 + 128],
                start=True, stop=True, tile_position=(64 * cc, 64 * par)),
                inc=("sT" if h == 15 else None))
        if cc == 0:
            wait("tensor", "sT", v)
    vR = v
    st[("solve_lastkqv", ti, pr)] = P.cnt["sT"]
    wait("vector", "sT", vPT)
    vPTe = op("vector", lambda e: e.tensor_sub(
        t["PT"][:, :].rearrange("p (hp a) -> p hp a", a=128),
        bass.AP(t["c_i64"], 0, [[128, 128], [0, 8], [1, 128]]),
        psA[:, :].rearrange("p (hp a) -> p hp a", a=128)), inc="sV")
    P.set_free("psA0", "sV", vPTe)
    P.set_free("psA1", "sV", vPTe)
    wait("scalar", "sT", vR)
    vRe = op("scalar", lambda e: e.copy(t["Rr"][:, :], psB[:, :]), inc="sS")
    P.set_free("psB", "sS", vRe)
    st[("pre_done", ti, pr)] = (vPTe, vRe, vQt, vOt)
    P.drive(1)


def emit_seq_chunk(P, t, ps, st, ti, ci):
    op, wait = P.op, P.wait
    pmA, pmB = ps["pmA"], ps["pmB"]
    pr = ci // 2
    g = ti * 4 + ci
    cur = (g % 2) * 512
    nxt = ((g + 1) % 2) * 512
    co = pr * 128 + (ci % 2) * 64
    cof = (ci % 2) * 64

    vPTe, vRe, vQt, vOt = st[("pre_done", ti, pr)]
    wait("tensor", "sV", vQt)
    wait("tensor", "sV", vPTe)
    wait("tensor", "sS", vRe)
    if st.get(("S_ready", g)) is not None:
        wait("tensor", "sV", st[("S_ready", g)])
    P.wait_free("tensor", "pmA")
    vO = None
    for h in range(16):
        par, hp = h % 2, h // 2
        vO = op("tensor", lambda e, h=h, par=par, hp=hp: e.matmul(
            pmA[par * 64:par * 64 + 64, hp * 64:hp * 64 + 64],
            t["S_bf"][par * 64:par * 64 + 64, cur + hp * 64:cur + hp * 64 + 64],
            t["Qt"][par * 64:par * 64 + 64,
                    hp * 128 + cof:hp * 128 + cof + 64],
            start=True, stop=True, tile_position=(64 * par, 64 * par)),
            inc=("sT" if h == 15 else None))
    P.wait_free("tensor", "pmB")
    vS = None
    for h in range(16):
        par, hp = h % 2, h // 2
        vS = op("tensor", lambda e, h=h, par=par, hp=hp: e.matmul(
            pmB[par * 64:par * 64 + 64, hp * 64:hp * 64 + 64],
            t["PT"][par * 64:par * 64 + 64,
                    hp * 128 + cof:hp * 128 + cof + 64],
            t["S_bf"][par * 64:par * 64 + 64, cur + hp * 64:cur + hp * 64 + 64],
            start=True, stop=True, tile_position=(64 * par, 64 * par)),
            inc=("sT" if h == 15 else None))
    st[("seq_mm_done", ti, ci)] = vS
    wait("vector", "sT", vO)
    wait("scalar", "sS", vOt)
    ev = op("vector", lambda e: e.tensor_add(
        t["O_tile"][:, :].rearrange("p (hp a) -> p hp a", a=256)[:, :, co:co + 64],
        pmA[:, :].rearrange("p (hp a) -> p hp a", a=64),
        t["Ot"][:, :].rearrange("p (hp a) -> p hp a", a=128)[:, :, cof:cof + 64]),
        inc="sV")
    P.set_free("pmA", "sV", ev)
    st[("O_evac", ti, ci)] = ev
    wait("vector", "sT", vS)
    ev = op("vector", lambda e: e.tensor_add(
        t["S_bf"][:, nxt:nxt + 512].rearrange("p (hp a) -> p hp a", a=64),
        pmB[:, :].rearrange("p (hp a) -> p hp a", a=64),
        t["Rr"][:, :].rearrange("p (hp a) -> p hp a", a=128)[:, :, cof:cof + 64]),
        inc="sV")
    P.set_free("pmB", "sV", ev)
    st[("S_ready", g + 1)] = ev


def emit_rms_wo_ln1(P, t, ps, dr, st, ti):
    op, wait = P.op, P.wait
    pmA, pmB, pMS = ps["pmA"], ps["pmB"], ps["pMS"]
    P.drive(1)
    wait("scalar", "sV", st[("O_evac", ti, 3)])
    wait("scalar", "sT", st[("kqv_lastuse", ti)])
    sqv = op("scalar", lambda e: e.activation(t["kb_a"][:, :], t["O_tile"][:, :],
                                              AF.Square), inc="sS")
    P.drive(1)
    wait("tensor", "sS", sqv)
    P.wait_free("tensor", "pMS")
    for cb in range(8):
        v = op("tensor", lambda e, cb=cb: e.matmul(
            pMS[0:16, 0:256], t["c_m16"][:, cb * 16:(cb + 1) * 16],
            t["kb_a"][:, cb * 256:(cb + 1) * 256],
            start=(cb == 0), stop=(cb == 7)), inc=("sT" if cb == 7 else None))
    st[("rms_stats", ti)] = v
    P.drive(1)
    wait("scalar", "sT", v)
    wait("scalar", "sV", st[(ti, "f", "qc")])
    vsq = op("scalar", lambda e: e.activation(
        t["nstat"][:, 256:512], pMS[0:16, 0:256], AF.Sqrt,
        bias=t["c_eps"][0:16, 0:1], scale=1.0 / 64.0), inc="sS")
    P.set_free("pMS", "sS", vsq)
    wait("vector", "sS", vsq)
    op("vector", lambda e: e.reciprocal(t["nstat"][:, 0:256], t["nstat"][:, 256:512]))
    vf = op("vector", lambda e: e.tensor_copy(t["frb"][:, :], t["nstat"][:, 0:256]),
            inc="sV")
    lastm = None
    for cb in range(8):
        half = (cb % 2) * 256
        wait("tensor", "sV", vf if cb == 0 else lastm)
        vbc = op("tensor", lambda e, cb=cb, half=half: e.matmul(
            pMS[:, half:half + 256], t["c_bm16"][:, cb * 128:(cb + 1) * 128],
            t["frb"][:, :], start=True, stop=True), inc="sT")
        wait("vector", "sT", vbc)
        lastm = op("vector", lambda e, cb=cb, half=half: e.tensor_mul(
            t["onorm"][:, cb * 256:(cb + 1) * 256],
            t["O_tile"][:, cb * 256:(cb + 1) * 256],
            pMS[:, half:half + 256]), inc="sV")
    st[("onorm_done", ti)] = lastm
    P.set_free("pMS", "sV", lastm)
    P.drive(2)

    wait("tensor", *st["wp_load"][(ti, 3)])
    wait("tensor", "sV", lastm)
    for mb in range(8):
        pm, pn = (pmA, "pmA") if mb % 2 == 0 else (pmB, "pmB")
        P.wait_free("tensor", pn)
        for kb in range(8):
            v = op("tensor", lambda e, pm=pm, kb=kb, mb=mb: e.matmul(
                pm[:, 0:256],
                t["wpB"][:, kb * 1024 + mb * 128:kb * 1024 + mb * 128 + 128],
                t["onorm"][:, kb * 256:(kb + 1) * 256],
                start=(kb == 0), stop=(kb == 7)), inc=("sT" if kb == 7 else None))
        wait("vector", "sT", v)
        ev = op("vector", lambda e, pm=pm, mb=mb: e.scalar_tensor_tensor(
            t["s_bf"][:, mb * 256:(mb + 1) * 256], pm[:, 0:256],
            t["w_bias"][:, 24 + mb:24 + mb + 1],
            t["x_bf"][:, mb * 256:(mb + 1) * 256],
            ALU.add, ALU.add), inc="sV")
        P.set_free(pn, "sV", ev)
    st[("s_done", ti)] = ev
    st[("wo_mm_done", ti)] = P.cnt["sT"]
    P.drive(2)
    if ti + 1 < NT:
        wait("sync", "sT", st[("wo_mm_done", ti)])
        st["wp_load"][(ti + 1, 1)] = ("dWPB", op(
            "sync", lambda e: e.dma_start(t["wpB"][:, :], dr["wq_bf"][:, :]),
            inc="dWPB"))

    P.drive(1)
    wait("scalar", "sV", ev)
    wait("scalar", "sT", st[("rms_stats", ti)])
    vsq = op("scalar", lambda e: e.activation(t["kb_a"][:, :], t["s_bf"][:, :],
                                              AF.Square), inc="sS")
    st[("ln1sq", ti)] = vsq
    P.drive_to(47)
    P.wait_free("tensor", "pMS")
    wait("tensor", "sV", st[("s_done", ti)])
    for cb in range(8):
        op("tensor", lambda e, cb=cb: e.matmul(
            pMS[0:1, 0:256], t["c_ones128"][:, 0:1],
            t["s_bf"][:, cb * 256:(cb + 1) * 256],
            start=(cb == 0), stop=(cb == 7)))
    wait("tensor", "sS", vsq)
    for cb in range(8):
        v = op("tensor", lambda e, cb=cb: e.matmul(
            pMS[0:1, 256:512], t["c_ones128"][:, 0:1],
            t["kb_a"][:, cb * 256:(cb + 1) * 256],
            start=(cb == 0), stop=(cb == 7)), inc=("sT" if cb == 7 else None))
    st[("ln1stats", ti)] = v
    emit_ln_tail(P, t, ps, st, ti, v, "ln1")

    wait("tensor", "sV", st[("ln1", ti)])
    P.wait_free("tensor", "pMS")
    vmu = op("tensor", lambda e: e.matmul(
        pMS[:, 0:256], t["c_bcast1"][:, :], t["frb"][0:1, 0:256],
        start=True, stop=True))
    vin = op("tensor", lambda e: e.matmul(
        pMS[:, 256:512], t["c_bcast1"][:, :], t["fqb"][0:1, 0:256],
        start=True, stop=True), inc="sT")
    wait("vector", "sT", vin)
    op("vector", lambda e: e.tensor_sub(
        t["y_bf"][:, :].rearrange("p (c a) -> p c a", a=256),
        t["s_bf"][:, :].rearrange("p (c a) -> p c a", a=256), _bc(pMS, 0)))
    vy = op("vector", lambda e: e.tensor_mul(
        t["y_bf"][:, :].rearrange("p (c a) -> p c a", a=256),
        t["y_bf"][:, :].rearrange("p (c a) -> p c a", a=256), _bc(pMS, 256)),
        inc="sV")
    st[("y_done", ti)] = vy
    P.set_free("pMS", "sV", vy)

    if ti + 1 < NT:
        wait("gpsimd", "sV", st[("s_done", ti)])
        st["x_cast"][ti + 1] = op("gpsimd", lambda e, ti=ti: e.dma_start(
            t["x_bf"][:, :], dr["x_img"][:, (ti + 1) * 2048:(ti + 2) * 2048]),
            inc="dG")


def emit_ln_tail(P, t, ps, st, ti, vstats, tag):
    op, wait = P.op, P.wait
    pMS = ps["pMS"]
    wait("vector", "sT", vstats)
    op("vector", lambda e: e.tensor_single_scalar(
        t["nstat"][0:1, 0:256], pMS[0:1, 0:256], 1.0 / D, ALU.mult))
    op("vector", lambda e: e.tensor_single_scalar(
        t["nstat"][0:1, 256:512], pMS[0:1, 256:512], 1.0 / D, ALU.mult))
    op("vector", lambda e: e.tensor_mul(
        t["nstat"][0:1, 512:768], t["nstat"][0:1, 0:256], t["nstat"][0:1, 0:256]))
    vv = op("vector", lambda e: e.tensor_sub(
        t["nstat"][0:1, 256:512], t["nstat"][0:1, 256:512],
        t["nstat"][0:1, 512:768]), inc="sV")
    P.set_free("pMS", "sV", vv)
    wait("scalar", "sV", vv)
    vs = op("scalar", lambda e: e.activation(
        t["nstat"][0:1, 512:768], t["nstat"][0:1, 256:512], AF.Sqrt,
        bias=t["c_eps"][0:1, 1:2]), inc="sS")
    wait("vector", "sS", vs)
    op("vector", lambda e: e.reciprocal(t["nstat"][0:1, 256:512],
                                        t["nstat"][0:1, 512:768]))
    op("vector", lambda e: e.tensor_copy(t["frb"][0:1, :], t["nstat"][0:1, 0:256]))
    v = op("vector", lambda e: e.tensor_copy(t["fqb"][0:1, :],
                                             t["nstat"][0:1, 256:512]), inc="sV")
    st[(tag, ti)] = v


def _bc(ph, col):
    return bass.AP(ph, col, [[512, 128], [0, 8], [1, 256]])


def ffn_chunks(P, t, ps, dr, st, ti):
    op, wait = P.op, P.wait
    pmA, pmB, pMS, pTR = ps["pmA"], ps["pmB"], ps["pMS"], ps["pTR"]
    vy = st[("y_done", ti)]
    yield

    P.wait_free("tensor", "pMS")
    wait("tensor", "sV", vy)
    if ti > 0:
        wait("scalar", "sT", st[("logits_lastuse", ti - 1)])
    for kb in range(8):
        v = op("tensor", lambda e, kb=kb: e.matmul(
            pMS[0:32, 0:256], t["w_router"][:, kb * 32:(kb + 1) * 32],
            t["y_bf"][:, kb * 256:(kb + 1) * 256],
            start=(kb == 0), stop=(kb == 7)), inc=("sT" if kb == 7 else None))
    wait("scalar", "sT", v)
    vlg = op("scalar", lambda e: e.activation(
        t["logits"][:, :], pMS[0:32, 0:256], AF.Identity,
        bias=t["w_rb"][:, 0:1]), inc="sS")
    yield
    wait("tensor", "sS", vlg)
    for j in range(2):
        v = op("tensor", lambda e, j=j: e.transpose(
            pMS[:, 256 + j * 32:256 + (j + 1) * 32],
            t["logits"][0:32, j * 128:(j + 1) * 128],
            t["c_identf"][0:32, 0:32]), inc=("sT" if j == 1 else None))
    st[("logits_lastuse", ti)] = v
    wait("vector", "sT", v)
    vcp = op("vector", lambda e: e.tensor_copy(t["lgTd"][:, 0:64], pMS[:, 256:320]),
             inc="sV")
    P.set_free("pMS", "sV", vcp)
    for j in range(2):
        op("vector", lambda e, j=j: e.reduce_max(
            t["lgmax"][:, j:j + 1], t["lgTd"][:, j * 32:(j + 1) * 32], AX.X))
    for j in range(2):
        v = op("vector", lambda e, j=j: e.tensor_single_scalar(
            t["maskTd"][:, j * 32:(j + 1) * 32], t["lgTd"][:, j * 32:(j + 1) * 32],
            t["lgmax"][:, j:j + 1], ALU.is_equal),
            inc=("sV" if j == 1 else None))
    yield
    wait("tensor", "sV", v)
    P.wait_free("tensor", "pTR")
    for j in range(2):
        v = op("tensor", lambda e, j=j: e.transpose(
            pTR[0:32, j * 128:(j + 1) * 128], t["maskTd"][:, j * 32:(j + 1) * 32],
            t["c_identb"][:, 0:128]), inc=("sT" if j == 1 else None))
    wait("vector", "sT", v)
    op("vector", lambda e: e.tensor_copy(t["mask_b"][:, :], pTR[0:32, 0:256]))
    vmask = op("vector", lambda e: e.tensor_single_scalar(
        t["premask"][:, :], t["mask_b"][:, :], 1.0, ALU.subtract), inc="sV")
    P.set_free("pTR", "sV", vmask)

    yield
    WSEM = ["dW1A", "dW1B", "dW2A"]
    for p0 in (0, 1):
        buf = f"wf{p0 % 3}"
        key = ("ffn2_mm", ti - 1, 7, 1) if p0 == 0 else ("ffn2_mm", ti - 1, 6, 1)
        if key in st:
            wait("sync", "sT", st[key])
        st["w1_load"][(ti, p0)] = (WSEM[p0 % 3], op(
            "sync", lambda e, p0=p0, buf=buf: e.dma_start(
                t[buf][:, :], dr["w1_bf"][:, p0 * 4096:(p0 + 1) * 4096]),
            inc=WSEM[p0 % 3]))
    wait("tensor", "sV", vmask)
    for ex in range(E):
        pr = ex // 2
        buf = f"wf{pr % 3}"
        if ex % 2 == 0:
            wait("tensor", *st["w1_load"][(ti, pr)])
        half = (ex % 2) * 2048
        vmm = None
        for mh in range(2):
            pm, pn = (pmA, "pmA") if mh == 0 else (pmB, "pmB")
            P.wait_free("tensor", pn)
            for kb in range(8):
                op("tensor", lambda e, pm=pm, buf=buf, kb=kb, mh=mh, half=half:
                   e.matmul(
                       pm[:, 0:256],
                       t[buf][:, half + kb * 256 + mh * 128:
                              half + kb * 256 + mh * 128 + 128],
                       t["y_bf"][:, kb * 256:(kb + 1) * 256],
                       start=(kb == 0), stop=False))
            vmm = op("tensor", lambda e, pm=pm, ex=ex: e.matmul(
                pm[:, 0:256], bass.AP(t["c_bigc"], ex, [[32, 32], [0, 128]]),
                t["premask"][:, :], start=False, stop=True), inc="sT")
            wait("scalar", "sT", vmm)
            ev = op("scalar", lambda e, pm=pm, ex=ex, mh=mh: e.activation(
                t["H_cat"][:, (2 * ex + mh) * 256:(2 * ex + mh + 1) * 256],
                pm[:, 0:256], AF.Relu,
                bias=t["w_b1"][:, 2 * ex + mh:2 * ex + mh + 1]), inc="sS")
            P.set_free(pn, "sS", ev)
        st[("ffn1_mm", ti, ex)] = vmm
        st[("ffn1_ev", ti, ex)] = ev
        yield
        if ex % 2 == 1 and pr + 2 < 16:
            pn2 = pr + 2
            if pr == 0:
                key = ("ffn2_mm", ti - 1, 7, 0)
                if key in st:
                    wait("sync", "sT", st[key])
            else:
                wait("sync", "sT", st[("ffn1_mm", ti, 2 * (pr - 1) + 1)])
            st["w1_load"][(ti, pn2)] = (WSEM[pn2 % 3], op(
                "sync", lambda e, pn2=pn2: e.dma_start(
                    t[f"wf{pn2 % 3}"][:, :],
                    dr["w1_bf"][:, pn2 * 4096:(pn2 + 1) * 4096]),
                inc=WSEM[pn2 % 3]))

    for j0 in (0, 1, 2):
        lastpair = {0: 15, 1: 13, 2: 14}[j0]
        wait("sync", "sT", st[("ffn1_mm", ti, 2 * lastpair + 1)])
        st["w2_load"][(ti, j0)] = (WSEM[j0 % 3], op(
            "sync", lambda e, j0=j0: e.dma_start(
                t[f"wf{j0 % 3}"][:, :], dr["w2_bf"][:, j0 * 4096:(j0 + 1) * 4096]),
            inc=WSEM[j0 % 3]))
    for mb in range(8):
        pm, pn = (pmA, "pmA") if mb % 2 == 0 else (pmB, "pmB")
        P.wait_free("tensor", pn)
        if mb == 0:
            wait("tensor", "sS", st[("ffn1_ev", ti, E - 1)])
        for hf in (0, 1):
            j = 2 * mb + hf
            buf = f"wf{j % 3}"
            wait("tensor", *st["w2_load"][(ti, j)])
            for kb in range(32):
                gkb = hf * 32 + kb
                v = op("tensor", lambda e, pm=pm, buf=buf, kb=kb, gkb=gkb, mb=mb: e.matmul(
                    pm[:, 0:256], t[buf][:, kb * 128:(kb + 1) * 128],
                    t["H_cat"][:, gkb * 256:(gkb + 1) * 256],
                    start=(gkb == 0), stop=False), inc=("sT" if kb == 31 else None))
            st[("ffn2_mm", ti, mb, hf)] = v
            if j + 3 < 16:
                wait("sync", "sT", v)
                st["w2_load"][(ti, j + 3)] = (WSEM[(j + 3) % 3], op(
                    "sync", lambda e, j=j: e.dma_start(
                        t[f"wf{(j + 3) % 3}"][:, :],
                        dr["w2_bf"][:, (j + 3) * 4096:(j + 4) * 4096]),
                    inc=WSEM[(j + 3) % 3]))
        vb2 = op("tensor", lambda e, pm=pm, mb=mb: e.matmul(
            pm[:, 0:256], t["w_b2"][:, mb * 128:(mb + 1) * 128],
            t["mask_b"][:, :], start=False, stop=True), inc="sT")
        wait("vector", "sT", vb2)
        if ti > 0 and mb == 0:
            wait("vector", "sS", st[("ln2sq", ti - 1)])
        ev = op("vector", lambda e, pm=pm, mb=mb: e.scalar_tensor_tensor(
            t["s2_bf"][:, mb * 256:(mb + 1) * 256], pm[:, 0:256], 0.0,
            t["y_bf"][:, mb * 256:(mb + 1) * 256], ALU.max, ALU.add), inc="sV")
        P.set_free(pn, "sV", ev)
        yield
    st[("s2_done", ti)] = ev

    wait("scalar", "sV", ev)
    vsq = op("scalar", lambda e: e.activation(t["qc"][:, :], t["s2_bf"][:, :],
                                              AF.Square), inc="sS")
    st[("ln2sq", ti)] = vsq
    P.wait_free("tensor", "pMS")
    wait("tensor", "sV", ev)
    for cb in range(8):
        op("tensor", lambda e, cb=cb: e.matmul(
            pMS[0:1, 0:256], t["c_ones128"][:, 0:1],
            t["s2_bf"][:, cb * 256:(cb + 1) * 256],
            start=(cb == 0), stop=(cb == 7)))
    wait("tensor", "sS", vsq)
    for cb in range(8):
        v = op("tensor", lambda e, cb=cb: e.matmul(
            pMS[0:1, 256:512], t["c_ones128"][:, 0:1],
            t["qc"][:, cb * 256:(cb + 1) * 256],
            start=(cb == 0), stop=(cb == 7)), inc=("sT" if cb == 7 else None))
    st[("ln2stats", ti)] = v
    yield
    emit_ln_tail(P, t, ps, st, ti, v, "ln2")
    wait("tensor", "sV", st[("ln2", ti)])
    P.wait_free("tensor", "pMS")
    vmu = op("tensor", lambda e: e.matmul(
        pMS[:, 0:256], t["c_bcast1"][:, :], t["frb"][0:1, 0:256],
        start=True, stop=True))
    vin = op("tensor", lambda e: e.matmul(
        pMS[:, 256:512], t["c_bcast1"][:, :], t["fqb"][0:1, 0:256],
        start=True, stop=True), inc="sT")
    wait("vector", "sT", vin)
    for hf in (0, 1):
        prev = st.get(("out_dma", ti, hf - 1)) if hf == 1 else (
            st.get(("out_dma", ti - 1, 1)) if ti > 0 else None)
        if prev is not None:
            wait("vector", "dS", prev)
        op("vector", lambda e, hf=hf: e.tensor_sub(
            t["out_f"][:, :].rearrange("p (c a) -> p c a", a=256),
            t["s2_bf"][:, hf * 1024:(hf + 1) * 1024].rearrange(
                "p (c a) -> p c a", a=256),
            bass.AP(pMS, 0, [[512, 128], [0, 4], [1, 256]])))
        vout = op("vector", lambda e, hf=hf: e.tensor_mul(
            t["out_f"][:, :].rearrange("p (c a) -> p c a", a=256),
            t["out_f"][:, :].rearrange("p (c a) -> p c a", a=256),
            bass.AP(pMS, 256, [[512, 128], [0, 4], [1, 256]])),
            inc="sV")
        wait("scalar", "sV", vout)
        st[("out_dma", ti, hf)] = op("scalar", lambda e, ti=ti, hf=hf: e.dma_start(
            dr["out_img"][:, ti * 2048 + hf * 1024:ti * 2048 + (hf + 1) * 1024],
            t["out_f"][:, :]), inc="dS")
        if hf == 1:
            P.set_free("pMS", "sV", vout)
        yield



def kernel(**inputs):
    imgs = _host_images(inputs)
    consts = _host_consts()
    x = np.asarray(inputs["L0_attn_x"], np.float32)
    nc = build_nc()
    in_maps = []
    for b in range(B):
        m = {}
        m.update(imgs)
        m.update({"d" + k: v for k, v in consts.items()})
        m["x_img"] = np.ascontiguousarray(
            x[b].reshape(NT, TILE, 8, 128).transpose(3, 0, 2, 1).reshape(128, 32768))
        in_maps.append(m)
    res = run_bass_kernel_spmd(nc, in_maps, core_ids=list(range(8)))
    outs = []
    for b in range(B):
        oi = np.asarray(res.results[b]["out_img"])
        outs.append(oi.reshape(128, NT, 8, TILE).transpose(1, 3, 2, 0).reshape(L, D))
    return np.stack(outs).astype(np.float32)


# version 12
# speedup vs baseline: 1.0670x; 1.0670x over previous
import numpy as np
import ml_dtypes
import concourse.bass as bass
import concourse.mybir as mybir
from concourse.bass_utils import run_bass_kernel_spmd
from contextlib import ExitStack

F32 = mybir.dt.float32
BF16 = mybir.dt.bfloat16
BF = ml_dtypes.bfloat16
AF = mybir.ActivationFunctionType
ALU = mybir.AluOpType
AX = mybir.AxisListType

B, L, D, H, HD = 8, 4096, 1024, 16, 64
TILE = 256
NT = L // TILE
NPAIR = 2
E, HH = 32, 256
BIG = 1e9
EPS = 1e-6
LNEPS = 1e-5



def _proj_img(w):
    return np.ascontiguousarray(
        w.T.reshape(8, 128, 8, 128).transpose(1, 0, 2, 3).reshape(128, 8192))


def _host_images(inputs):
    g = lambda n: np.asarray(inputs[n], np.float32)
    imgs = {}
    imgs["wq_img"] = _proj_img(g("L0_attn_Wq"))
    imgs["wk_img"] = _proj_img(g("L0_attn_Wk"))
    imgs["wv_img"] = _proj_img(g("L0_attn_Wv"))
    imgs["wo_img"] = _proj_img(g("L0_attn_Wo"))
    wb = g("L0_attn_Wbeta")
    imgs["wbeta_img"] = np.ascontiguousarray(
        wb.T.reshape(8, 128, 16).transpose(1, 0, 2).reshape(128, 128))
    rw = g("L0_ffn_router_w")
    imgs["router_img"] = np.ascontiguousarray(
        rw.T.reshape(8, 128, 32).transpose(1, 0, 2).reshape(128, 256))
    w1 = g("L0_ffn_weights1")
    imgs["w1_img"] = np.ascontiguousarray(
        w1.reshape(E, 8, 128, 2, 128).transpose(2, 0, 1, 3, 4).reshape(128, 65536))
    w2 = g("L0_ffn_weights2").reshape(E * HH, D)
    imgs["w2_img"] = np.ascontiguousarray(
        w2.reshape(64, 128, 8, 128).transpose(1, 2, 0, 3).reshape(128, 65536))
    bias = np.zeros((128, 32), np.float32)
    for j, n in enumerate(["L0_attn_bk", "L0_attn_bq", "L0_attn_bv", "L0_attn_bo"]):
        bias[:, j * 8:(j + 1) * 8] = g(n).reshape(8, 128).T
    imgs["bias_img"] = bias
    cw = np.zeros((128, 96), np.float32)
    for j, n in enumerate(["L0_attn_wk_conv", "L0_attn_wq_conv", "L0_attn_wv_conv"]):
        cw[:, j * 32:(j + 1) * 32] = g(n).reshape(8, 128, 4).transpose(1, 0, 2).reshape(128, 32)
    imgs["convw_img"] = cw
    cb = np.zeros((128, 24), np.float32)
    for j, n in enumerate(["L0_attn_bk_conv", "L0_attn_bq_conv", "L0_attn_bv_conv"]):
        cb[:, j * 8:(j + 1) * 8] = g(n).reshape(8, 128).T
    imgs["convb_img"] = cb
    imgs["b1_img"] = np.ascontiguousarray(
        g("L0_ffn_bias1").reshape(E, 2, 128).transpose(2, 0, 1).reshape(128, 64))
    imgs["b2_img"] = np.ascontiguousarray(g("L0_ffn_bias2"))
    imgs["rb_img"] = np.ascontiguousarray(g("L0_ffn_router_b").reshape(32, 1))
    imgs["bbeta_img"] = np.ascontiguousarray(g("L0_attn_bbeta").reshape(16, 1))
    imgs["nw_img"] = np.ascontiguousarray(np.tile(g("L0_attn_norm_w"), 2).reshape(128, 1))
    return imgs


def _host_consts():
    c = {}
    c["c_identb"] = np.eye(128, dtype=BF)
    c["c_identf"] = np.eye(32, dtype=np.float32)
    sl = np.zeros((128, 128), np.float32)
    for cc in range(2):
        sl[cc * 64:(cc + 1) * 64, cc * 64:(cc + 1) * 64] = np.tril(
            np.ones((64, 64), np.float32), -1)
    c["c_slneg"] = np.tile(-sl, (1, 8)).astype(BF)
    su = np.zeros((128, 128), np.float32)
    for cc in range(2):
        su[cc * 64:(cc + 1) * 64, cc * 64:(cc + 1) * 64] = np.triu(
            np.ones((64, 64), np.float32), 1)
    c["c_sun"] = np.tile(-su, (1, 8)).astype(BF)
    iu = np.zeros((128, 128), np.float32)
    for cc in range(2):
        iu[cc * 64:(cc + 1) * 64, cc * 64:(cc + 1) * 64] = np.triu(
            np.ones((64, 64), np.float32), 0)
    c["c_iu"] = iu.astype(BF)
    i64 = np.eye(64, dtype=np.float32)
    i2 = np.zeros((128, 128), np.float32)
    for a in (0, 64):
        for b_ in (0, 64):
            i2[a:a + 64, b_:b_ + 64] = i64
    c["c_i64"] = i2
    m16 = np.zeros((128, 128), np.float32)
    for cb in range(8):
        for p in range(128):
            m16[p, cb * 16 + 2 * cb + p // 64] = 1.0
    c["c_m16"] = m16.astype(BF)
    bm = np.zeros((16, 1024), np.float32)
    for cb in range(8):
        for p in range(128):
            bm[2 * cb + p // 64, cb * 128 + p] = 1.0
    c["c_bm16"] = bm.astype(BF)
    c["c_ones128"] = np.ones((128, 1), BF)
    c["c_bcast1"] = np.ones((1, 128), BF)
    c["c_bigc"] = (np.eye(32, dtype=np.float32) * BIG).astype(BF)
    epst = np.zeros((128, 4), np.float32)
    epst[:, 0] = EPS
    epst[:, 1] = LNEPS
    c["c_eps"] = epst
    return c



ENGS = ("tensor", "vector", "scalar", "gpsimd", "sync")


SYNC_EVERY = 3
PARANOID_T = False
AUTOSEM = {"vector": "aV", "scalar": "aS", "gpsimd": "aG"}


class Prog:
    def __init__(self):
        self.ops = {e: [] for e in ENGS}
        self.cnt = {}
        self.free = {}
        self.chain = {e: 0 for e in ENGS}

    def op(self, eng, build, inc=None):
        auto = None
        is_dma = inc is not None and inc.startswith("d")
        if eng in AUTOSEM and not is_dma:
            self.chain[eng] += 1
            if self.chain[eng] >= SYNC_EVERY:
                self.chain[eng] = 0
                if inc is None:
                    inc = AUTOSEM[eng]
                else:
                    auto = True
        if eng == "tensor" and PARANOID_T and inc is None:
            inc = "sT"
            auto = True
        if inc is not None:
            amt = 16 if inc.startswith("d") else 1
            if amt == 16:
                self.wait(eng, inc, self.cnt.get(inc, 0))
            self.cnt[inc] = self.cnt.get(inc, 0) + amt
            val = self.cnt[inc]

            def fn(e, sems, build=build, inc=inc, amt=amt):
                build(e).then_inc(sems[inc], amt)
            self.ops[eng].append(fn)
            if eng in AUTOSEM and (auto or inc == AUTOSEM[eng]):
                self.wait(eng, inc, val)
            if eng == "tensor" and PARANOID_T:
                self.wait(eng, inc, val)
            return val
        self.ops[eng].append(lambda e, sems, build=build: build(e))
        return None

    def wait(self, eng, name, val):
        if val is None or val <= 0:
            return
        self.ops[eng].append(
            lambda e, sems, name=name, val=val: e.wait_ge(sems[name], val))

    def drive(self, n=1):
        if getattr(self, "ffn_gen", None) is None:
            return
        for _ in range(n):
            if next(self.ffn_gen, "END") == "END":
                self.ffn_gen = None
                return
            self.gen_yields = getattr(self, "gen_yields", 0) + 1

    def drive_to(self, n):
        while (getattr(self, "ffn_gen", None) is not None
               and getattr(self, "gen_yields", 0) < n):
            self.drive(1)

    def drive_all(self):
        while getattr(self, "ffn_gen", None) is not None:
            self.drive(64)

    def wait_free(self, eng, res):
        if res in self.free:
            s, v = self.free[res]
            self.wait(eng, s, v)

    def set_free(self, res, sem, val):
        self.free[res] = (sem, val)



CONSTS = [("c_identb", [128, 128], BF16), ("c_identf", [32, 32], F32),
          ("c_slneg", [128, 1024], BF16), ("c_sun", [128, 1024], BF16), ("c_iu", [128, 128], BF16),
          ("c_i64", [128, 128], F32), ("c_m16", [128, 128], BF16),
          ("c_bm16", [16, 1024], BF16), ("c_ones128", [128, 1], BF16),
          ("c_bcast1", [1, 128], BF16), ("c_bigc", [32, 32], BF16),
          ("c_eps", [128, 4], F32)]


def build_nc():
    nc = bass.Bass("TRN2", target_bir_lowering=False, detect_race_conditions=False)
    dr = {}
    dr["x_img"] = nc.dram_tensor("x_img", [128, 32768], F32, kind="ExternalInput")
    for n in ["wq_img", "wk_img", "wv_img", "wo_img"]:
        dr[n] = nc.dram_tensor(n, [128, 8192], F32, kind="ExternalInput")
    dr["wbeta_img"] = nc.dram_tensor("wbeta_img", [128, 128], F32, kind="ExternalInput")
    dr["router_img"] = nc.dram_tensor("router_img", [128, 256], F32, kind="ExternalInput")
    dr["w1_img"] = nc.dram_tensor("w1_img", [128, 65536], F32, kind="ExternalInput")
    dr["w2_img"] = nc.dram_tensor("w2_img", [128, 65536], F32, kind="ExternalInput")
    for n, sh in [("bias_img", [128, 32]), ("convw_img", [128, 96]),
                  ("convb_img", [128, 24]), ("b1_img", [128, 64]),
                  ("b2_img", [32, 1024]), ("rb_img", [32, 1]),
                  ("bbeta_img", [16, 1]), ("nw_img", [128, 1])]:
        dr[n] = nc.dram_tensor(n, sh, F32, kind="ExternalInput")
    for n, sh, dt in CONSTS:
        dr[n] = nc.dram_tensor("d" + n, sh, dt, kind="ExternalInput")
    dr["out_img"] = nc.dram_tensor("out_img", [128, 32768], F32, kind="ExternalOutput")
    for n in ["wq_bf", "wk_bf", "wv_bf", "wo_bf"]:
        dr[n] = nc.dram_tensor(n, [128, 8192], BF16, kind="Internal")
    dr["w1_bf"] = nc.dram_tensor("w1_bf", [128, 65536], BF16, kind="Internal")
    dr["w2_bf"] = nc.dram_tensor("w2_bf", [128, 65536], BF16, kind="Internal")

    es = ExitStack()

    def sbt(name, shape, dtype):
        return es.enter_context(nc.sbuf_tensor(name, shape, dtype))

    t = {}
    for n, sh, dt in CONSTS:
        t[n] = sbt(n, sh, dt)
    t["w_beta"] = sbt("w_beta", [128, 128], BF16)
    t["w_router"] = sbt("w_router", [128, 256], BF16)
    t["w_b2"] = sbt("w_b2", [32, 1024], BF16)
    t["w_nw"] = sbt("w_nw", [128, 1], F32)
    t["w_bias"] = sbt("w_bias", [128, 32], F32)
    t["w_cw"] = sbt("w_cw", [128, 96], F32)
    t["w_cb"] = sbt("w_cb", [128, 24], F32)
    t["w_b1"] = sbt("w_b1", [128, 64], F32)
    t["w_rb"] = sbt("w_rb", [32, 1], F32)
    t["w_bbeta"] = sbt("w_bbeta", [16, 1], F32)
    t["wpA"] = sbt("wpA", [128, 8192], BF16)
    t["wpB"] = sbt("wpB", [128, 8192], BF16)
    t["wf0"] = sbt("wf0", [128, 4096], BF16)
    t["wf1"] = sbt("wf1", [128, 4096], BF16)
    t["wf2"] = sbt("wf2", [128, 4096], BF16)
    t["H_cat"] = sbt("H_cat", [128, 16384], BF16)
    t["x_bf"] = sbt("x_bf", [128, 2048], BF16)
    for n in ["kp", "qp", "vp"]:
        t[n] = sbt(n, [128, 8 * 259], BF16)
    for n in ["kc", "qc", "k_n", "q_n", "v_n", "kb_a", "vb_a"]:
        t[n] = sbt(n, [128, 2048], BF16)
    t["beta_b"] = sbt("beta_b", [16, 256], BF16)
    t["T_st"] = sbt("T_st", [128, 1024], F32)
    t["U_bd"] = sbt("U_bd", [128, 2048], F32)
    t["Y0"] = sbt("Y0", [128, 2048], F32)
    t["Ybf"] = sbt("Ybf", [128, 2048], BF16)
    t["MT"] = sbt("MT", [128, 2048], BF16)
    t["Kst"] = sbt("Kst", [128, 1024], BF16)
    t["Qt"] = sbt("Qt", [128, 1024], BF16)
    t["Ot"] = sbt("Ot", [128, 1024], BF16)
    t["PT"] = sbt("PT", [128, 1024], BF16)
    t["Rr"] = sbt("Rr", [128, 1024], BF16)
    t["S_bf"] = sbt("S_bf", [128, 1024], BF16)
    t["O_tile"] = t["qc"]
    t["onorm"] = t["kc"]
    t["s_bf"] = sbt("s_bf", [128, 2048], BF16)
    t["vc"] = t["vb_a"]
    t["y_bf"] = sbt("y_bf", [128, 2048], BF16)
    t["s2_bf"] = sbt("s2_bf", [128, 2048], BF16)
    t["out_f"] = sbt("out_f", [128, 1024], F32)
    t["logits"] = sbt("logits", [32, 256], F32)
    t["lgTd"] = sbt("lgTd", [128, 64], F32)
    t["lgmax"] = sbt("lgmax", [128, 2], F32)
    t["maskTd"] = sbt("maskTd", [128, 64], BF16)
    t["mask_b"] = sbt("mask_b", [32, 256], BF16)
    t["premask"] = sbt("premask", [32, 256], BF16)
    t["nstat"] = sbt("nstat", [16, 768], F32)
    t["fkb"] = sbt("fkb", [16, 256], BF16)
    t["fqb"] = sbt("fqb", [16, 256], BF16)
    t["frb"] = sbt("frb", [16, 256], BF16)

    ps = {}
    ps["pmA"] = es.enter_context(nc.psum_tensor("pmA", [128, 512], F32))
    ps["pmB"] = es.enter_context(nc.psum_tensor("pmB", [128, 512], F32))
    ps["psA"] = es.enter_context(nc.psum_tensor("psA", [128, 1024], F32))
    ps["psB"] = es.enter_context(nc.psum_tensor("psB", [128, 1024], F32))
    ps["pTR"] = es.enter_context(nc.psum_tensor("pTR", [128, 1024], BF16))
    ps["pMS"] = es.enter_context(nc.psum_tensor("pMS", [128, 512], F32))

    sems = {}
    for n in ["sT", "sV", "sS", "sG", "sY", "aV", "aS", "aG", "dY", "dS", "dG",
              "dWPA", "dWPB", "dW1A", "dW1B", "dW2A", "dW2B"]:
        sems[n] = es.enter_context(nc.semaphore(n))

    P = Prog()
    st = {"wp_load": {}, "w1_load": {}, "w2_load": {}, "x_cast": {}}
    emit_init(P, t, ps, dr, st)
    P.ffn_gen = None
    for ti in range(NT):
        emit_p1(P, t, ps, dr, st, ti)
        emit_p2(P, t, ps, dr, st, ti)
        emit_p3(P, t, ps, dr, st, ti)
        for pr in range(NPAIR):
            emit_solve_pair(P, t, ps, st, ti, pr)
            for half in range(2):
                P.drive(1)
                emit_seq_chunk(P, t, ps, st, ti, pr * 2 + half)
        st[("kqv_lastuse", ti)] = st[("solve_lastkqv", ti, 1)]
        emit_rms_wo_ln1(P, t, ps, dr, st, ti)
        P.drive_all()
        P.ffn_gen = ffn_chunks(P, t, ps, dr, st, ti)
        P.gen_yields = 0
    P.drive_all()
    P.wait("sync", "dS", P.cnt.get("dS", 0))

    with nc.Block() as block:
        @block.tensor
        def _(e):
            for fn in P.ops["tensor"]:
                fn(e, sems)

        @block.vector
        def _(e):
            for fn in P.ops["vector"]:
                fn(e, sems)

        @block.scalar
        def _(e):
            for fn in P.ops["scalar"]:
                fn(e, sems)

        @block.gpsimd
        def _(e):
            for fn in P.ops["gpsimd"]:
                fn(e, sems)

        @block.sync
        def _(e):
            for fn in P.ops["sync"]:
                fn(e, sems)

    es.close()
    return nc


def emit_init(P, t, ps, dr, st):
    op, wait = P.op, P.wait
    for n, _, _ in CONSTS:
        op("sync", lambda e, n=n: e.dma_start(t[n][:, :], dr[n][:, :]), inc="dY")
    for src, dst in [("bias_img", "w_bias"), ("convw_img", "w_cw"),
                     ("convb_img", "w_cb"), ("b1_img", "w_b1"),
                     ("rb_img", "w_rb"), ("bbeta_img", "w_bbeta"),
                     ("nw_img", "w_nw")]:
        op("sync", lambda e, s=src, d=dst: e.dma_start(t[d][:, :], dr[s][:, :]), inc="dY")
    wait("sync", "dY", P.cnt["dY"])
    st["init_Y"] = op("sync", lambda e: e.nop(), inc="sY")
    op("gpsimd", lambda e: e.dma_start(t["w_beta"][:, :], dr["wbeta_img"][:, :]), inc="dG")
    op("gpsimd", lambda e: e.dma_start(t["w_router"][:, :], dr["router_img"][:, :]), inc="dG")
    op("gpsimd", lambda e: e.dma_start(t["w_b2"][:, :], dr["b2_img"][:, :]), inc="dG")
    wait("gpsimd", "dG", P.cnt["dG"])

    op("gpsimd", lambda e: e.memset(t["S_bf"][:, :], 0.0))
    op("gpsimd", lambda e: e.memset(t["U_bd"][:, :], 0.0))
    for n in ["kp", "qp", "vp"]:
        op("gpsimd", lambda e, n=n: e.memset(t[n][:, :], 0.0))
    st["init_G"] = op("gpsimd", lambda e: e.nop(), inc="sG")

    for img, scr in [("wk_img", "wk_bf"), ("wq_img", "wq_bf"), ("wv_img", "wv_bf")]:
        v = op("gpsimd", lambda e, img=img: e.dma_start(t["wpA"][:, :], dr[img][:, :]),
               inc="dG")
        wait("sync", "dG", v)
        v2 = op("sync", lambda e, scr=scr: e.dma_start(dr[scr][:, :], t["wpA"][:, :]),
                inc="dY")
        wait("gpsimd", "dY", v2)
    v = op("gpsimd", lambda e: e.dma_start(t["wpA"][:, :], dr["wo_img"][:, :]), inc="dG")
    wait("vector", "dG", v)
    wait("vector", "sY", st["init_Y"])
    v = op("vector", lambda e: e.tensor_single_scalar(
        t["wpB"][:, :], t["wpA"][:, :], t["w_nw"][:, 0:1], ALU.mult), inc="sV")
    wait("sync", "sV", v)
    op("sync", lambda e: e.dma_start(dr["wo_bf"][:, :], t["wpB"][:, :]), inc="dY")

    prev_store = None
    for img, scr in [("w1_img", "w1_bf"), ("w2_img", "w2_bf")]:
        for ch in range(8):
            half = (ch % 2) * 8192
            if prev_store is not None:
                wait("gpsimd", "dY", prev_store)
            v = op("gpsimd", lambda e, img=img, ch=ch, half=half: e.dma_start(
                t["H_cat"][:, half:half + 8192],
                dr[img][:, ch * 8192:(ch + 1) * 8192]), inc="dG")
            wait("sync", "dG", v)
            prev_store = op("sync", lambda e, scr=scr, ch=ch, half=half: e.dma_start(
                dr[scr][:, ch * 8192:(ch + 1) * 8192],
                t["H_cat"][:, half:half + 8192]), inc="dY")

    st["wp_load"][(0, 0)] = ("dWPA", op(
        "sync", lambda e: e.dma_start(t["wpA"][:, :], dr["wk_bf"][:, :]), inc="dWPA"))
    st["wp_load"][(0, 1)] = ("dWPB", op(
        "sync", lambda e: e.dma_start(t["wpB"][:, :], dr["wq_bf"][:, :]), inc="dWPB"))
    st["x_cast"][0] = op("gpsimd", lambda e: e.dma_start(
        t["x_bf"][:, :], dr["x_img"][:, 0:2048]), inc="dG")
    st[("S_ready", 0)] = None


def _engs(P, ps):
    return (ps[n] for n in ["pmA", "pmB", "psA", "psB", "pTR", "pMS"])


def emit_p1(P, t, ps, dr, st, ti):
    op, wait = P.op, P.wait
    pmA, pmB, psA, psB, pTR, pMS = _engs(P, ps)

    wait("tensor", "dG", st["x_cast"][ti])
    if ti == 0:
        wait("tensor", "sY", st["init_Y"])
        wait("tensor", "sG", st["init_G"])
        wait("tensor", "sG", st["init_G"])
    projs = [("kp", "wpA", 0), ("qp", "wpB", 8), ("vp", "wpA", 16)]
    for pi, (arena, wbuf, bcol) in enumerate(projs):
        wait("tensor", *st["wp_load"][(ti, pi)])
        if ti > 0 and pi == 0:
            wait("tensor", "sV", st[("conv_done", ti - 1)])
        if ti > 0 and pi == 2:
            wait("tensor", "sV", st[("conv_done", ti - 1)])
        for mb in range(8):
            half = (mb % 2) * 512
            pn = "psA0" if mb % 2 == 0 else "psA1"
            P.wait_free("tensor", pn)
            for kb in range(8):
                v = op("tensor", lambda e, half=half, wbuf=wbuf, kb=kb, mb=mb: e.matmul(
                    psA[:, half:half + 256],
                    t[wbuf][:, kb * 1024 + mb * 128:kb * 1024 + mb * 128 + 128],
                    t["x_bf"][:, kb * 256:(kb + 1) * 256],
                    start=(kb == 0), stop=(kb == 7)),
                    inc=("sT" if kb == 7 else None))
            wait("scalar", "sT", v)
            if ti == 0 and pi == 0 and mb == 0:
                wait("scalar", "sY", st["init_Y"])
            if ti > 0 and mb == 0 and pi == 0:
                wait("scalar", "sV", st[("conv_done", ti - 1)])
            if ti > 0 and mb == 0 and pi == 2:
                wait("scalar", "sV", st[("conv_done", ti - 1)])
            ev = op("scalar", lambda e, half=half, arena=arena, mb=mb, bcol=bcol:
                    e.activation(t[arena][:, mb * 259 + 3:mb * 259 + 259],
                                 psA[:, half:half + 256], AF.Identity,
                                 bias=t["w_bias"][:, bcol + mb:bcol + mb + 1]),
                    inc="sS")
            P.set_free(pn, "sS", ev)
        st[("proj_done", ti, pi)] = ev
        st[("projmm", ti, pi)] = v
        wait("sync", "sT", v)
        if pi == 0:
            st["wp_load"][(ti, 2)] = ("dWPA", op(
                "sync", lambda e: e.dma_start(t["wpA"][:, :], dr["wv_bf"][:, :]),
                inc="dWPA"))
        elif pi == 1:
            st["wp_load"][(ti, 3)] = ("dWPB", op(
                "sync", lambda e: e.dma_start(t["wpB"][:, :], dr["wo_bf"][:, :]),
                inc="dWPB"))
        elif ti + 1 < NT:
            st["wp_load"][(ti + 1, 0)] = ("dWPA", op(
                "sync", lambda e: e.dma_start(t["wpA"][:, :], dr["wk_bf"][:, :]),
                inc="dWPA"))
        P.drive(1)
    st[("proj_mm_done", ti)] = P.cnt["sT"]

    P.wait_free("tensor", "pMS")
    if ti > 0:
        wait("scalar", "sT", st[("beta_lastuse", ti - 1)])
    for kb in range(8):
        v = op("tensor", lambda e, kb=kb: e.matmul(
            pMS[0:16, 0:256], t["w_beta"][:, kb * 16:(kb + 1) * 16],
            t["x_bf"][:, kb * 256:(kb + 1) * 256],
            start=(kb == 0), stop=(kb == 7)), inc=("sT" if kb == 7 else None))
    wait("scalar", "sT", v)
    ev = op("scalar", lambda e: e.activation(t["beta_b"][:, :], pMS[0:16, 0:256],
                                             AF.Sigmoid, bias=t["w_bbeta"][:, 0:1]),
            inc="sS")
    st[("beta", ti)] = ev
    P.set_free("pMS", "sS", ev)


def emit_p2(P, t, ps, dr, st, ti):
    op, wait = P.op, P.wait
    pmA, pmB, psA, psB, pTR, pMS = _engs(P, ps)
    for pi, (arena, cout, eng) in enumerate([("kp", "kc", "vector"),
                                             ("qp", "qc", "vector"),
                                             ("vp", "vc", "vector")]):
        wait(eng, "sS", st[("proj_done", ti, pi)])
        if ti == 0:
            wait(eng, "sY", st["init_Y"])
            wait(eng, "sG", st["init_G"])
        if ti > 0:
            wait(eng, "sS", st[("sq_done", ti - 1)])
            wait(eng, "sV", st[("p3mul_done", ti - 1)])
        sem = "sV" if eng == "vector" else "sG"
        for cb in range(8):
            base = cb * 259
            wc = pi * 32 + cb * 4
            op(eng, lambda e, arena=arena, cout=cout, base=base, wc=wc, cb=cb, pi=pi:
               e.tensor_scalar(t[cout][:, cb * 256:(cb + 1) * 256],
                               t[arena][:, base + 0:base + 256],
                               t["w_cw"][:, wc:wc + 1],
                               t["w_cb"][:, pi * 8 + cb:pi * 8 + cb + 1],
                               ALU.mult, ALU.add))
            for tap in (1, 2, 3):
                op(eng, lambda e, arena=arena, cout=cout, base=base, wc=wc, cb=cb, tap=tap:
                   e.scalar_tensor_tensor(t[cout][:, cb * 256:(cb + 1) * 256],
                                          t[arena][:, base + tap:base + tap + 256],
                                          t["w_cw"][:, wc + tap:wc + tap + 1],
                                          t[cout][:, cb * 256:(cb + 1) * 256],
                                          ALU.mult, ALU.add))
        ev = op(eng, lambda e, arena=arena: e.tensor_copy(
            t[arena][:, :].rearrange("p (cb a) -> p cb a", a=259)[:, :, 0:3],
            t[arena][:, :].rearrange("p (cb a) -> p cb a", a=259)[:, :, 256:259]),
            inc=sem)
        st[("conv", ti, pi)] = (sem, ev)
        P.drive(2)
    st[("conv_done", ti)] = P.cnt["sV"]
    st[("conv_done_g", ti)] = P.cnt["sG"]


def emit_p3(P, t, ps, dr, st, ti):
    op, wait = P.op, P.wait
    pmA, pmB, psA, psB, pTR, pMS = _engs(P, ps)
    if ti > 0:
        wait("vector", "sT", st[("kqv_lastuse", ti - 1)])
        wait("scalar", "sT", st[("ln1stats", ti - 1)])
    for src, col0, fb in [("kc", 0, "fkb"), ("qc", 256, "fqb")]:
        wait("scalar", "sV", st[("conv_done", ti)])
        if src == "qc":
            wait("scalar", "sT", st[(ti, "kred")])
        sqv = op("scalar", lambda e, src=src: e.activation(
            t["kb_a"][:, :], t[src][:, :], AF.Square), inc="sS")
        wait("tensor", "sS", sqv)
        P.wait_free("tensor", "pMS")
        for cb in range(8):
            v = op("tensor", lambda e, cb=cb: e.matmul(
                pMS[0:16, 0:256], t["c_m16"][:, cb * 16:(cb + 1) * 16],
                t["kb_a"][:, cb * 256:(cb + 1) * 256],
                start=(cb == 0), stop=(cb == 7)), inc=("sT" if cb == 7 else None))
        st[(ti, "kred")] = v
        wait("scalar", "sT", v)
        sq2 = op("scalar", lambda e, col0=col0: e.activation(
            t["nstat"][:, col0:col0 + 256], pMS[0:16, 0:256], AF.Sqrt), inc="sS")
        P.set_free("pMS", "sS", sq2)
        wait("vector", "sS", sq2)
        op("vector", lambda e, col0=col0: e.tensor_single_scalar(
            t["nstat"][:, 512:768],
            t["nstat"][:, col0:col0 + 256], EPS, ALU.add))
        op("vector", lambda e, col0=col0: e.reciprocal(
            t["nstat"][:, col0:col0 + 256], t["nstat"][:, 512:768]))
        st[(ti, "f", src)] = op("vector", lambda e, fb=fb, col0=col0: e.tensor_copy(
            t[fb][:, :], t["nstat"][:, col0:col0 + 256]), inc="sV")
    wait("scalar", "sV", st[("conv_done", ti)])
    wait("scalar", "sT", st[(ti, "kred")])
    vsg = op("scalar", lambda e: e.activation(t["kb_a"][:, :], t["vc"][:, :],
                                              AF.Sigmoid), inc="sS")
    st[("sq_done", ti)] = vsg
    P.drive(1)

    psA, psB = ps["psA"], ps["psB"]
    P.wait_free("tensor", "psA0")
    P.wait_free("tensor", "psA1")
    P.wait_free("tensor", "psB")

    def reg(i):
        pp = psA if i % 4 < 2 else psB
        c0 = (i % 2) * 512
        return pp[:, c0:c0 + 256]

    regmul = [None] * 4
    mi = 0
    lastmul = None
    for cb in range(8):
        P.drive(1)
        wait("tensor", "sV", st[(ti, "f", "kc")])
        wait("tensor", "sV", regmul[mi % 4])
        vK = op("tensor", lambda e, cb=cb, mi=mi: e.matmul(
            reg(mi), t["c_bm16"][:, cb * 128:(cb + 1) * 128],
            t["fkb"][:, :], start=True, stop=True), inc="sT")
        wait("vector", "sT", vK)
        op("vector", lambda e, cb=cb, mi=mi: e.tensor_mul(
            t["k_n"][:, cb * 256:(cb + 1) * 256], t["kc"][:, cb * 256:(cb + 1) * 256],
            reg(mi)))
        wait("vector", "sS", vsg)
        op("vector", lambda e, cb=cb: e.tensor_mul(
            t["v_n"][:, cb * 256:(cb + 1) * 256], t["vc"][:, cb * 256:(cb + 1) * 256],
            t["kb_a"][:, cb * 256:(cb + 1) * 256]))
        regmul[mi % 4] = op("vector", lambda e, cb=cb, mi=mi: e.tensor_mul(
            t["v_n"][:, cb * 256:(cb + 1) * 256], t["v_n"][:, cb * 256:(cb + 1) * 256],
            reg(mi)), inc="sV")
        mi += 1
        wait("tensor", "sV", st[(ti, "f", "qc")])
        wait("tensor", "sV", regmul[mi % 4])
        vQ = op("tensor", lambda e, cb=cb, mi=mi: e.matmul(
            reg(mi), t["c_bm16"][:, cb * 128:(cb + 1) * 128],
            t["fqb"][:, :], start=True, stop=True), inc="sT")
        wait("vector", "sT", vQ)
        regmul[mi % 4] = op("vector", lambda e, cb=cb, mi=mi: e.tensor_mul(
            t["q_n"][:, cb * 256:(cb + 1) * 256], t["qc"][:, cb * 256:(cb + 1) * 256],
            reg(mi)), inc="sV")
        mi += 1
        wait("tensor", "sS", st[("beta", ti)])
        wait("tensor", "sV", regmul[mi % 4])
        vB = op("tensor", lambda e, cb=cb, mi=mi: e.matmul(
            reg(mi), t["c_bm16"][:, cb * 128:(cb + 1) * 128],
            t["beta_b"][:, :], start=True, stop=True), inc="sT")
        wait("vector", "sT", vB)
        op("vector", lambda e, cb=cb, mi=mi: e.tensor_mul(
            t["kb_a"][:, cb * 256:(cb + 1) * 256], t["k_n"][:, cb * 256:(cb + 1) * 256],
            reg(mi)))
        lastmul = op("vector", lambda e, cb=cb, mi=mi: e.tensor_mul(
            t["vb_a"][:, cb * 256:(cb + 1) * 256], t["v_n"][:, cb * 256:(cb + 1) * 256],
            reg(mi)), inc="sV")
        regmul[mi % 4] = lastmul
        mi += 1
    st[("qkv_done", ti)] = P.cnt["sV"]
    st[("p3mul_done", ti)] = P.cnt["sV"]
    st[("beta_lastuse", ti)] = P.cnt["sT"]
    P.set_free("psA0", "sV", lastmul)
    P.set_free("psA1", "sV", lastmul)
    P.set_free("psB", "sV", lastmul)


def emit_solve_pair(P, t, ps, st, ti, pr):
    op, wait = P.op, P.wait
    psA, psB, pTR = ps["psA"], ps["psB"], ps["pTR"]
    po = pr * 128

    def sl(arena, h, w0, wn):
        par, hp = h % 2, h // 2
        return t[arena][par * 64:par * 64 + 64,
                        hp * 256 + po + w0:hp * 256 + po + w0 + wn]

    def solve16(dst_maker, lhs_maker, rhs_maker, tag):
        order = [h for h in range(16) if h % 2 == 0] + \
                [h for h in range(16) if h % 2 == 1]
        vA = vB = None
        for i, h in enumerate(order):
            pp = psA if h < 8 else psB
            last_even = (i == 7)
            v = op("tensor",
                   lambda e, h=h, pp=pp: e.matmul(
                       dst_maker(pp, h), lhs_maker(h), rhs_maker(h),
                       start=True, stop=True,
                       tile_position=(64 * (h % 2), 0)),
                   inc=("sT" if (last_even or i == 15) else None))
            if last_even:
                wait("tensor", "sT", v)
            if i == 15:
                vA = vB = v
        return vA, vB

    wait("tensor", "sV", st[("qkv_done", ti)])
    P.wait_free("tensor", "psA0")
    P.wait_free("tensor", "psA1")
    P.wait_free("tensor", "psB")

    vA, vB = solve16(lambda pp, h: pp[:, (h % 8) * 128:(h % 8) * 128 + 128],
                     lambda h: sl("kb_a", h, 0, 128),
                     lambda h: sl("k_n", h, 0, 128), "G")
    wait("vector", "sT", vB)
    for (pp, hs) in [(psA, 0), (psB, 8)]:
        for part, c0 in [(0, 0), (64, 64)]:
            vT0 = op("vector", lambda e, pp=pp, hs=hs, part=part, c0=c0: e.tensor_tensor(
                t["T_st"][part:part + 64, :].rearrange("p (h a) -> p h a", a=64)
                [:, hs:hs + 8, :],
                pp[part:part + 64, :].rearrange("p (h a) -> p h a", a=128)[:, :, c0:c0 + 64],
                t["c_slneg"][part:part + 64, :].rearrange("p (h a) -> p h a", a=128)
                [:, 0:8, c0:c0 + 64],
                ALU.mult), inc=("sV" if (hs == 8 and part == 64) else None))

    wait("tensor", "sV", vT0)
    vA, vB = solve16(lambda pp, h: pp[:, (h % 8) * 128:(h % 8) * 128 + 128],
                     lambda h: sl("k_n", h, 0, 128),
                     lambda h: sl("kb_a", h, 0, 128), "U0")
    wait("vector", "sT", vB)
    op("vector", lambda e: e.tensor_tensor(
        t["U_bd"][0:64, :].rearrange("p (h a) -> p h a", a=128)[:, 0:8, 0:64],
        psA[0:64, :].rearrange("p (h a) -> p h a", a=128)[:, :, 0:64],
        t["c_slneg"][0:64, :].rearrange("p (h a) -> p h a", a=128)[:, 0:8, 0:64] if False else
        t["c_sun"][0:64, :].rearrange("p (h a) -> p h a", a=128)[:, 0:8, 0:64],
        ALU.mult))
    op("vector", lambda e: e.tensor_tensor(
        t["U_bd"][64:128, :].rearrange("p (h a) -> p h a", a=128)[:, 0:8, 64:128],
        psA[64:128, :].rearrange("p (h a) -> p h a", a=128)[:, :, 64:128],
        t["c_sun"][64:128, :].rearrange("p (h a) -> p h a", a=128)[:, 0:8, 64:128],
        ALU.mult))
    op("vector", lambda e: e.tensor_tensor(
        t["U_bd"][0:64, :].rearrange("p (h a) -> p h a", a=128)[:, 8:16, 0:64],
        psB[0:64, :].rearrange("p (h a) -> p h a", a=128)[:, :, 0:64],
        t["c_sun"][0:64, :].rearrange("p (h a) -> p h a", a=128)[:, 0:8, 0:64],
        ALU.mult))
    vU0 = op("vector", lambda e: e.tensor_tensor(
        t["U_bd"][64:128, :].rearrange("p (h a) -> p h a", a=128)[:, 8:16, 64:128],
        psB[64:128, :].rearrange("p (h a) -> p h a", a=128)[:, :, 64:128],
        t["c_sun"][64:128, :].rearrange("p (h a) -> p h a", a=128)[:, 0:8, 64:128],
        ALU.mult), inc="sV")

    wait("tensor", "sV", vU0)
    vA, vB = solve16(lambda pp, h: pp[:, (h % 8) * 128:(h % 8) * 128 + 128],
                     lambda h: sl("k_n", h, 0, 128),
                     lambda h: sl("q_n", h, 0, 128), "MT")
    wait("vector", "sT", vA)
    op("vector", lambda e: e.tensor_tensor(
        t["MT"][:, 0:1024].rearrange("p (h a) -> p h a", a=128),
        psA[:, :].rearrange("p (h a) -> p h a", a=128),
        bass.AP(t["c_iu"], 0, [[128, 128], [0, 8], [1, 128]]), ALU.mult))
    wait("vector", "sT", vB)
    vMT = op("vector", lambda e: e.tensor_tensor(
        t["MT"][:, 1024:2048].rearrange("p (h a) -> p h a", a=128),
        psB[:, :].rearrange("p (h a) -> p h a", a=128),
        bass.AP(t["c_iu"], 0, [[128, 128], [0, 8], [1, 128]]), ALU.mult), inc="sV")
    st[("MT_done", ti, pr)] = vMT
    P.drive(1)

    if (ti, pr) != (0, 0):
        P.wait_free("tensor", "pTR")
    prev_ev = None
    for gi, (src, dst, dof) in enumerate([("k_n", "Kst", None),
                                          ("kb_a", "Y0", 0), ("vb_a", "Y0", 64)]):
        if prev_ev is not None:
            wait("tensor", "sS", prev_ev)
        order = [h for h in range(16) if h % 2 == 0] + \
                [h for h in range(16) if h % 2 == 1]
        for i, h in enumerate(order):
            pb = (h % 2) * 64
            v = op("tensor", lambda e, src=src, h=h, pb=pb: e.transpose(
                pTR[:, h * 64:(h + 1) * 64], sl(src, h, 0, 128),
                t["c_identb"][pb:pb + 64, pb:pb + 64],
                tile_position=(pb, 0)),
                inc=("sT" if i in (7, 15) else None))
            if i == 7:
                wait("tensor", "sT", v)
        wait("scalar", "sT", v)
        if dof is None:
            prev_ev = op("scalar", lambda e: e.copy(
                t["Kst"][:, :], pTR[:, :]), inc="sS")
        else:
            prev_ev = op("scalar", lambda e, dof=dof: e.copy(
                t["Y0"][:, :].rearrange("p (h a) -> p h a", a=128)[:, :, dof:dof + 64],
                pTR[:, :].rearrange("p (h a) -> p h a", a=64)[:, :, :]), inc="sS")
    vY0 = prev_ev

    P.set_free("pTR", "sS", vY0)
    vU = ("sS", vY0)

    for mlev in range(6):
        wait("tensor", *vU)
        if mlev == 0:
            wait("tensor", "sV", vU0)
        vA = vB = None
        for h in range(16):
            pp = psA if h < 8 else psB
            hc = (h % 8) * 128
            v = op("tensor", lambda e, h=h, pp=pp, hc=hc: e.matmul(
                pp[:, hc:hc + 128], t["U_bd"][:, h * 128:(h + 1) * 128],
                t["Y0"][:, h * 128:(h + 1) * 128], start=True, stop=True),
                inc=("sT" if h in (7, 15) else None))
            if h == 7:
                vA = v
            else:
                vB = v
        wait("vector", "sT", vA)
        op("vector", lambda e: e.tensor_add(t["Y0"][:, 0:1024], t["Y0"][:, 0:1024],
                                            psA[:, :]))
        wait("vector", "sT", vB)
        vY = op("vector", lambda e: e.tensor_add(t["Y0"][:, 1024:2048],
                                                 t["Y0"][:, 1024:2048], psB[:, :]),
                inc="sV")
        P.drive(1)
        if mlev == 5:
            wait("scalar", "sV", vY)
            op("scalar", lambda e: e.copy(t["Ybf"][:, 0:1024], t["Y0"][:, 0:1024]))
            vYb = op("scalar", lambda e: e.copy(t["Ybf"][:, 1024:2048],
                                                t["Y0"][:, 1024:2048]), inc="sS")
            st[("Y_done", ti, pr)] = vY
            st[("Ybf_done", ti, pr)] = vYb
            break
        wait("tensor", "sV", vY)
        for h in range(16):
            v = op("tensor", lambda e, h=h: e.matmul(
                psA[:, h * 64:(h + 1) * 64], t["U_bd"][:, h * 128:(h + 1) * 128],
                t["T_st"][:, h * 64:(h + 1) * 64], start=True, stop=True),
                inc=("sT" if h == 15 else None))
        vTn = v
        for h in range(16):
            for cc in range(2):
                v = op("tensor", lambda e, h=h, cc=cc: e.matmul(
                    psB[cc * 64:cc * 64 + 64, h * 64:(h + 1) * 64],
                    t["T_st"][cc * 64:cc * 64 + 64, h * 64:(h + 1) * 64],
                    t["U_bd"][cc * 64:cc * 64 + 64,
                              h * 128 + cc * 64:h * 128 + cc * 64 + 64],
                    start=True, stop=True,
                    tile_position=(cc * 64, cc * 64)),
                    inc=("sT" if (h, cc) == (15, 1) else None))
        vUn = v
        wait("scalar", "sT", vTn)
        wait("scalar", "sT", vUn)
        op("scalar", lambda e: e.copy(t["T_st"][:, :], psA[:, :]))
        op("scalar", lambda e: e.copy(
            t["U_bd"][0:64, :].rearrange("p (h a) -> p h a", a=128)[:, :, 0:64],
            psB[0:64, :].rearrange("p (h a) -> p h a", a=64)[:, :, :]))
        vU = ("sS", op("scalar", lambda e: e.copy(
            t["U_bd"][64:128, :].rearrange("p (h a) -> p h a", a=128)[:, :, 64:128],
            psB[64:128, :].rearrange("p (h a) -> p h a", a=64)[:, :, :]), inc="sS"))

    wait("tensor", "sV", st[("Y_done", ti, pr)])
    wait("tensor", "sS", st[("Ybf_done", ti, pr)])
    if ti > 0 or pr > 0:
        key = ("seq_mm_done", ti - 1, 3) if pr == 0 else ("seq_mm_done", ti, 1)
        if key in st:
            pass
    vA = vB = None
    for h in range(16):
        par, hp = h % 2, h // 2
        v = op("tensor", lambda e, h=h, par=par, hp=hp: e.matmul(
            psA[par * 64:par * 64 + 64, hp * 128:hp * 128 + 128],
            t["Ybf"][:, h * 128:h * 128 + 64],
            t["MT"][:, h * 128:(h + 1) * 128],
            start=True, stop=True, tile_position=(0, 64 * par)),
            inc=("sT" if h == 15 else None))
    vMW = v
    for h in range(16):
        par, hp = h % 2, h // 2
        v = op("tensor", lambda e, h=h, par=par, hp=hp: e.matmul(
            psB[par * 64:par * 64 + 64, hp * 128:hp * 128 + 128],
            t["Ybf"][:, h * 128 + 64:h * 128 + 128],
            t["MT"][:, h * 128:(h + 1) * 128],
            start=True, stop=True, tile_position=(0, 64 * par)),
            inc=("sT" if h == 15 else None))
    vMU = v
    prevseq = st.get(("seq_mm_done", ti, 1)) if pr == 1 else (
        st.get(("seq_mm_done", ti - 1, 3)) if ti > 0 else None)
    wait("vector", "sT", vMW)
    if prevseq is not None:
        wait("vector", "sT", prevseq)
        wait("scalar", "sT", prevseq)
    vQt = op("vector", lambda e: e.tensor_sub(
        t["Qt"][:, :],
        t["q_n"][:, :].rearrange("p (hp a) -> p hp a", a=256)[:, :, po:po + 128],
        psA[:, :].rearrange("p (hp a) -> p hp a", a=128)), inc="sV")
    wait("scalar", "sT", vMU)
    vOt = op("scalar", lambda e: e.copy(
        t["Ot"][:, :], psB[:, :]), inc="sS")
    wait("tensor", "sV", vQt)
    for cc in range(2):
        for h in range(16):
            par, hp = h % 2, h // 2
            v = op("tensor", lambda e, h=h, par=par, hp=hp, cc=cc: e.matmul(
                psA[par * 64:par * 64 + 64,
                    hp * 128 + cc * 64:hp * 128 + cc * 64 + 64],
                t["Ybf"][cc * 64:cc * 64 + 64, h * 128:h * 128 + 64],
                t["Kst"][cc * 64:cc * 64 + 64, h * 64:(h + 1) * 64],
                start=True, stop=True, tile_position=(64 * cc, 64 * par)),
                inc=("sT" if h == 15 else None))
        if cc == 0:
            wait("tensor", "sT", v)
    vPT = v
    wait("tensor", "sS", vOt)
    for cc in range(2):
        for h in range(16):
            par, hp = h % 2, h // 2
            v = op("tensor", lambda e, h=h, par=par, hp=hp, cc=cc: e.matmul(
                psB[par * 64:par * 64 + 64,
                    hp * 128 + cc * 64:hp * 128 + cc * 64 + 64],
                t["Kst"][cc * 64:cc * 64 + 64, h * 64:(h + 1) * 64],
                t["Ybf"][cc * 64:cc * 64 + 64, h * 128 + 64:h * 128 + 128],
                start=True, stop=True, tile_position=(64 * cc, 64 * par)),
                inc=("sT" if h == 15 else None))
        if cc == 0:
            wait("tensor", "sT", v)
    vR = v
    st[("solve_lastkqv", ti, pr)] = P.cnt["sT"]
    wait("vector", "sT", vPT)
    vPTe = op("vector", lambda e: e.tensor_sub(
        t["PT"][:, :].rearrange("p (hp a) -> p hp a", a=128),
        bass.AP(t["c_i64"], 0, [[128, 128], [0, 8], [1, 128]]),
        psA[:, :].rearrange("p (hp a) -> p hp a", a=128)), inc="sV")
    P.set_free("psA0", "sV", vPTe)
    P.set_free("psA1", "sV", vPTe)
    wait("scalar", "sT", vR)
    vRe = op("scalar", lambda e: e.copy(t["Rr"][:, :], psB[:, :]), inc="sS")
    P.set_free("psB", "sS", vRe)
    st[("pre_done", ti, pr)] = (vPTe, vRe, vQt, vOt)
    P.drive(1)


def emit_seq_chunk(P, t, ps, st, ti, ci):
    op, wait = P.op, P.wait
    pmA, pmB = ps["pmA"], ps["pmB"]
    pr = ci // 2
    g = ti * 4 + ci
    cur = (g % 2) * 512
    nxt = ((g + 1) % 2) * 512
    co = pr * 128 + (ci % 2) * 64
    cof = (ci % 2) * 64

    vPTe, vRe, vQt, vOt = st[("pre_done", ti, pr)]
    wait("tensor", "sV", vQt)
    wait("tensor", "sV", vPTe)
    wait("tensor", "sS", vRe)
    if st.get(("S_ready", g)) is not None:
        wait("tensor", "sV", st[("S_ready", g)])
    P.wait_free("tensor", "pmA")
    vO = None
    for h in range(16):
        par, hp = h % 2, h // 2
        vO = op("tensor", lambda e, h=h, par=par, hp=hp: e.matmul(
            pmA[par * 64:par * 64 + 64, hp * 64:hp * 64 + 64],
            t["S_bf"][par * 64:par * 64 + 64, cur + hp * 64:cur + hp * 64 + 64],
            t["Qt"][par * 64:par * 64 + 64,
                    hp * 128 + cof:hp * 128 + cof + 64],
            start=True, stop=True, tile_position=(64 * par, 64 * par)),
            inc=("sT" if h == 15 else None))
    P.wait_free("tensor", "pmB")
    vS = None
    for h in range(16):
        par, hp = h % 2, h // 2
        vS = op("tensor", lambda e, h=h, par=par, hp=hp: e.matmul(
            pmB[par * 64:par * 64 + 64, hp * 64:hp * 64 + 64],
            t["PT"][par * 64:par * 64 + 64,
                    hp * 128 + cof:hp * 128 + cof + 64],
            t["S_bf"][par * 64:par * 64 + 64, cur + hp * 64:cur + hp * 64 + 64],
            start=True, stop=True, tile_position=(64 * par, 64 * par)),
            inc=("sT" if h == 15 else None))
    st[("seq_mm_done", ti, ci)] = vS
    wait("vector", "sT", vO)
    wait("scalar", "sS", vOt)
    ev = op("vector", lambda e: e.tensor_add(
        t["O_tile"][:, :].rearrange("p (hp a) -> p hp a", a=256)[:, :, co:co + 64],
        pmA[:, :].rearrange("p (hp a) -> p hp a", a=64),
        t["Ot"][:, :].rearrange("p (hp a) -> p hp a", a=128)[:, :, cof:cof + 64]),
        inc="sV")
    P.set_free("pmA", "sV", ev)
    st[("O_evac", ti, ci)] = ev
    wait("vector", "sT", vS)
    ev = op("vector", lambda e: e.tensor_add(
        t["S_bf"][:, nxt:nxt + 512].rearrange("p (hp a) -> p hp a", a=64),
        pmB[:, :].rearrange("p (hp a) -> p hp a", a=64),
        t["Rr"][:, :].rearrange("p (hp a) -> p hp a", a=128)[:, :, cof:cof + 64]),
        inc="sV")
    P.set_free("pmB", "sV", ev)
    st[("S_ready", g + 1)] = ev


def emit_rms_wo_ln1(P, t, ps, dr, st, ti):
    op, wait = P.op, P.wait
    pmA, pmB, pMS = ps["pmA"], ps["pmB"], ps["pMS"]
    P.drive(1)
    wait("scalar", "sV", st[("O_evac", ti, 3)])
    wait("scalar", "sT", st[("kqv_lastuse", ti)])
    sqv = op("scalar", lambda e: e.activation(t["kb_a"][:, :], t["O_tile"][:, :],
                                              AF.Square), inc="sS")
    P.drive(1)
    wait("tensor", "sS", sqv)
    P.wait_free("tensor", "pMS")
    for cb in range(8):
        v = op("tensor", lambda e, cb=cb: e.matmul(
            pMS[0:16, 0:256], t["c_m16"][:, cb * 16:(cb + 1) * 16],
            t["kb_a"][:, cb * 256:(cb + 1) * 256],
            start=(cb == 0), stop=(cb == 7)), inc=("sT" if cb == 7 else None))
    st[("rms_stats", ti)] = v
    P.drive(1)
    wait("scalar", "sT", v)
    wait("scalar", "sV", st[(ti, "f", "qc")])
    vsq = op("scalar", lambda e: e.activation(
        t["nstat"][:, 256:512], pMS[0:16, 0:256], AF.Sqrt,
        bias=t["c_eps"][0:16, 0:1], scale=1.0 / 64.0), inc="sS")
    P.set_free("pMS", "sS", vsq)
    wait("vector", "sS", vsq)
    op("vector", lambda e: e.reciprocal(t["nstat"][:, 0:256], t["nstat"][:, 256:512]))
    vf = op("vector", lambda e: e.tensor_copy(t["frb"][:, :], t["nstat"][:, 0:256]),
            inc="sV")
    lastm = None
    for cb in range(8):
        half = (cb % 2) * 256
        wait("tensor", "sV", vf if cb == 0 else lastm)
        vbc = op("tensor", lambda e, cb=cb, half=half: e.matmul(
            pMS[:, half:half + 256], t["c_bm16"][:, cb * 128:(cb + 1) * 128],
            t["frb"][:, :], start=True, stop=True), inc="sT")
        wait("vector", "sT", vbc)
        lastm = op("vector", lambda e, cb=cb, half=half: e.tensor_mul(
            t["onorm"][:, cb * 256:(cb + 1) * 256],
            t["O_tile"][:, cb * 256:(cb + 1) * 256],
            pMS[:, half:half + 256]), inc="sV")
    st[("onorm_done", ti)] = lastm
    P.set_free("pMS", "sV", lastm)
    P.drive(2)

    wait("tensor", *st["wp_load"][(ti, 3)])
    wait("tensor", "sV", lastm)
    for mb in range(8):
        pm, pn = (pmA, "pmA") if mb % 2 == 0 else (pmB, "pmB")
        P.wait_free("tensor", pn)
        for kb in range(8):
            v = op("tensor", lambda e, pm=pm, kb=kb, mb=mb: e.matmul(
                pm[:, 0:256],
                t["wpB"][:, kb * 1024 + mb * 128:kb * 1024 + mb * 128 + 128],
                t["onorm"][:, kb * 256:(kb + 1) * 256],
                start=(kb == 0), stop=(kb == 7)), inc=("sT" if kb == 7 else None))
        wait("vector", "sT", v)
        ev = op("vector", lambda e, pm=pm, mb=mb: e.scalar_tensor_tensor(
            t["s_bf"][:, mb * 256:(mb + 1) * 256], pm[:, 0:256],
            t["w_bias"][:, 24 + mb:24 + mb + 1],
            t["x_bf"][:, mb * 256:(mb + 1) * 256],
            ALU.add, ALU.add), inc="sV")
        P.set_free(pn, "sV", ev)
    st[("s_done", ti)] = ev
    st[("wo_mm_done", ti)] = P.cnt["sT"]
    P.drive(2)
    if ti + 1 < NT:
        wait("sync", "sT", st[("wo_mm_done", ti)])
        st["wp_load"][(ti + 1, 1)] = ("dWPB", op(
            "sync", lambda e: e.dma_start(t["wpB"][:, :], dr["wq_bf"][:, :]),
            inc="dWPB"))

    P.drive(1)
    wait("scalar", "sV", ev)
    wait("scalar", "sT", st[("rms_stats", ti)])
    vsq = op("scalar", lambda e: e.activation(t["kb_a"][:, :], t["s_bf"][:, :],
                                              AF.Square), inc="sS")
    st[("ln1sq", ti)] = vsq
    P.drive_to(47)
    P.wait_free("tensor", "pMS")
    wait("tensor", "sV", st[("s_done", ti)])
    for cb in range(8):
        op("tensor", lambda e, cb=cb: e.matmul(
            pMS[0:1, 0:256], t["c_ones128"][:, 0:1],
            t["s_bf"][:, cb * 256:(cb + 1) * 256],
            start=(cb == 0), stop=(cb == 7)))
    wait("tensor", "sS", vsq)
    for cb in range(8):
        v = op("tensor", lambda e, cb=cb: e.matmul(
            pMS[0:1, 256:512], t["c_ones128"][:, 0:1],
            t["kb_a"][:, cb * 256:(cb + 1) * 256],
            start=(cb == 0), stop=(cb == 7)), inc=("sT" if cb == 7 else None))
    st[("ln1stats", ti)] = v
    emit_ln_tail(P, t, ps, st, ti, v, "ln1")

    wait("tensor", "sV", st[("ln1", ti)])
    P.wait_free("tensor", "pMS")
    vmu = op("tensor", lambda e: e.matmul(
        pMS[:, 0:256], t["c_bcast1"][:, :], t["frb"][0:1, 0:256],
        start=True, stop=True))
    vin = op("tensor", lambda e: e.matmul(
        pMS[:, 256:512], t["c_bcast1"][:, :], t["fqb"][0:1, 0:256],
        start=True, stop=True), inc="sT")
    wait("vector", "sT", vin)
    op("vector", lambda e: e.tensor_sub(
        t["y_bf"][:, :].rearrange("p (c a) -> p c a", a=256),
        t["s_bf"][:, :].rearrange("p (c a) -> p c a", a=256), _bc(pMS, 0)))
    vy = op("vector", lambda e: e.tensor_mul(
        t["y_bf"][:, :].rearrange("p (c a) -> p c a", a=256),
        t["y_bf"][:, :].rearrange("p (c a) -> p c a", a=256), _bc(pMS, 256)),
        inc="sV")
    st[("y_done", ti)] = vy
    P.set_free("pMS", "sV", vy)

    if ti + 1 < NT:
        wait("gpsimd", "sV", st[("s_done", ti)])
        st["x_cast"][ti + 1] = op("gpsimd", lambda e, ti=ti: e.dma_start(
            t["x_bf"][:, :], dr["x_img"][:, (ti + 1) * 2048:(ti + 2) * 2048]),
            inc="dG")


def emit_ln_tail(P, t, ps, st, ti, vstats, tag):
    op, wait = P.op, P.wait
    pMS = ps["pMS"]
    wait("vector", "sT", vstats)
    op("vector", lambda e: e.tensor_single_scalar(
        t["nstat"][0:1, 0:256], pMS[0:1, 0:256], 1.0 / D, ALU.mult))
    op("vector", lambda e: e.tensor_single_scalar(
        t["nstat"][0:1, 256:512], pMS[0:1, 256:512], 1.0 / D, ALU.mult))
    op("vector", lambda e: e.tensor_mul(
        t["nstat"][0:1, 512:768], t["nstat"][0:1, 0:256], t["nstat"][0:1, 0:256]))
    vv = op("vector", lambda e: e.tensor_sub(
        t["nstat"][0:1, 256:512], t["nstat"][0:1, 256:512],
        t["nstat"][0:1, 512:768]), inc="sV")
    P.set_free("pMS", "sV", vv)
    wait("scalar", "sV", vv)
    vs = op("scalar", lambda e: e.activation(
        t["nstat"][0:1, 512:768], t["nstat"][0:1, 256:512], AF.Sqrt,
        bias=t["c_eps"][0:1, 1:2]), inc="sS")
    wait("vector", "sS", vs)
    op("vector", lambda e: e.reciprocal(t["nstat"][0:1, 256:512],
                                        t["nstat"][0:1, 512:768]))
    op("vector", lambda e: e.tensor_copy(t["frb"][0:1, :], t["nstat"][0:1, 0:256]))
    v = op("vector", lambda e: e.tensor_copy(t["fqb"][0:1, :],
                                             t["nstat"][0:1, 256:512]), inc="sV")
    st[(tag, ti)] = v


def _bc(ph, col):
    return bass.AP(ph, col, [[512, 128], [0, 8], [1, 256]])


def ffn_chunks(P, t, ps, dr, st, ti):
    op, wait = P.op, P.wait
    pmA, pmB, pMS, pTR = ps["pmA"], ps["pmB"], ps["pMS"], ps["pTR"]
    vy = st[("y_done", ti)]
    yield

    P.wait_free("tensor", "pMS")
    wait("tensor", "sV", vy)
    if ti > 0:
        wait("scalar", "sT", st[("logits_lastuse", ti - 1)])
    for kb in range(8):
        v = op("tensor", lambda e, kb=kb: e.matmul(
            pMS[0:32, 0:256], t["w_router"][:, kb * 32:(kb + 1) * 32],
            t["y_bf"][:, kb * 256:(kb + 1) * 256],
            start=(kb == 0), stop=(kb == 7)), inc=("sT" if kb == 7 else None))
    wait("scalar", "sT", v)
    vlg = op("scalar", lambda e: e.activation(
        t["logits"][:, :], pMS[0:32, 0:256], AF.Identity,
        bias=t["w_rb"][:, 0:1]), inc="sS")
    yield
    wait("tensor", "sS", vlg)
    for j in range(2):
        v = op("tensor", lambda e, j=j: e.transpose(
            pMS[:, 256 + j * 32:256 + (j + 1) * 32],
            t["logits"][0:32, j * 128:(j + 1) * 128],
            t["c_identf"][0:32, 0:32]), inc=("sT" if j == 1 else None))
    st[("logits_lastuse", ti)] = v
    wait("vector", "sT", v)
    vcp = op("vector", lambda e: e.tensor_copy(t["lgTd"][:, 0:64], pMS[:, 256:320]),
             inc="sV")
    P.set_free("pMS", "sV", vcp)
    for j in range(2):
        op("vector", lambda e, j=j: e.reduce_max(
            t["lgmax"][:, j:j + 1], t["lgTd"][:, j * 32:(j + 1) * 32], AX.X))
    for j in range(2):
        v = op("vector", lambda e, j=j: e.tensor_single_scalar(
            t["maskTd"][:, j * 32:(j + 1) * 32], t["lgTd"][:, j * 32:(j + 1) * 32],
            t["lgmax"][:, j:j + 1], ALU.is_equal),
            inc=("sV" if j == 1 else None))
    yield
    wait("tensor", "sV", v)
    P.wait_free("tensor", "pTR")
    for j in range(2):
        v = op("tensor", lambda e, j=j: e.transpose(
            pTR[0:32, j * 128:(j + 1) * 128], t["maskTd"][:, j * 32:(j + 1) * 32],
            t["c_identb"][:, 0:128]), inc=("sT" if j == 1 else None))
    wait("vector", "sT", v)
    op("vector", lambda e: e.tensor_copy(t["mask_b"][:, :], pTR[0:32, 0:256]))
    vmask = op("vector", lambda e: e.tensor_single_scalar(
        t["premask"][:, :], t["mask_b"][:, :], 1.0, ALU.subtract), inc="sV")
    P.set_free("pTR", "sV", vmask)

    yield
    WSEM = ["dW1A", "dW1B", "dW2A"]
    for p0 in (0, 1):
        buf = f"wf{p0 % 3}"
        key = ("ffn2_mm", ti - 1, 7, 1) if p0 == 0 else ("ffn2_mm", ti - 1, 6, 1)
        if key in st:
            wait("sync", "sT", st[key])
        st["w1_load"][(ti, p0)] = (WSEM[p0 % 3], op(
            "sync", lambda e, p0=p0, buf=buf: e.dma_start(
                t[buf][:, :], dr["w1_bf"][:, p0 * 4096:(p0 + 1) * 4096]),
            inc=WSEM[p0 % 3]))
    wait("tensor", "sV", vmask)
    for ex in range(E):
        pr = ex // 2
        buf = f"wf{pr % 3}"
        if ex % 2 == 0:
            wait("tensor", *st["w1_load"][(ti, pr)])
        half = (ex % 2) * 2048
        vmm = None
        for mh in range(2):
            pm, pn = (pmA, "pmA") if mh == 0 else (pmB, "pmB")
            P.wait_free("tensor", pn)
            for kb in range(8):
                op("tensor", lambda e, pm=pm, buf=buf, kb=kb, mh=mh, half=half:
                   e.matmul(
                       pm[:, 0:256],
                       t[buf][:, half + kb * 256 + mh * 128:
                              half + kb * 256 + mh * 128 + 128],
                       t["y_bf"][:, kb * 256:(kb + 1) * 256],
                       start=(kb == 0), stop=False))
            vmm = op("tensor", lambda e, pm=pm, ex=ex: e.matmul(
                pm[:, 0:256], bass.AP(t["c_bigc"], ex, [[32, 32], [0, 128]]),
                t["premask"][:, :], start=False, stop=True), inc="sT")
            wait("scalar", "sT", vmm)
            ev = op("scalar", lambda e, pm=pm, ex=ex, mh=mh: e.activation(
                t["H_cat"][:, (2 * ex + mh) * 256:(2 * ex + mh + 1) * 256],
                pm[:, 0:256], AF.Relu,
                bias=t["w_b1"][:, 2 * ex + mh:2 * ex + mh + 1]), inc="sS")
            P.set_free(pn, "sS", ev)
        st[("ffn1_mm", ti, ex)] = vmm
        st[("ffn1_ev", ti, ex)] = ev
        yield
        if ex % 2 == 1 and pr + 2 < 16:
            pn2 = pr + 2
            if pr == 0:
                key = ("ffn2_mm", ti - 1, 7, 0)
                if key in st:
                    wait("sync", "sT", st[key])
            else:
                wait("sync", "sT", st[("ffn1_mm", ti, 2 * (pr - 1) + 1)])
            st["w1_load"][(ti, pn2)] = (WSEM[pn2 % 3], op(
                "sync", lambda e, pn2=pn2: e.dma_start(
                    t[f"wf{pn2 % 3}"][:, :],
                    dr["w1_bf"][:, pn2 * 4096:(pn2 + 1) * 4096]),
                inc=WSEM[pn2 % 3]))

    for j0 in (0, 1, 2):
        lastpair = {0: 15, 1: 13, 2: 14}[j0]
        wait("sync", "sT", st[("ffn1_mm", ti, 2 * lastpair + 1)])
        st["w2_load"][(ti, j0)] = (WSEM[j0 % 3], op(
            "sync", lambda e, j0=j0: e.dma_start(
                t[f"wf{j0 % 3}"][:, :], dr["w2_bf"][:, j0 * 4096:(j0 + 1) * 4096]),
            inc=WSEM[j0 % 3]))
    for mb in range(8):
        pm, pn = (pmA, "pmA") if mb % 2 == 0 else (pmB, "pmB")
        P.wait_free("tensor", pn)
        if mb == 0:
            wait("tensor", "sS", st[("ffn1_ev", ti, E - 1)])
        for hf in (0, 1):
            j = 2 * mb + hf
            buf = f"wf{j % 3}"
            wait("tensor", *st["w2_load"][(ti, j)])
            for kb in range(32):
                gkb = hf * 32 + kb
                v = op("tensor", lambda e, pm=pm, buf=buf, kb=kb, gkb=gkb, mb=mb: e.matmul(
                    pm[:, 0:256], t[buf][:, kb * 128:(kb + 1) * 128],
                    t["H_cat"][:, gkb * 256:(gkb + 1) * 256],
                    start=(gkb == 0), stop=False), inc=("sT" if kb == 31 else None))
            st[("ffn2_mm", ti, mb, hf)] = v
            if j + 3 < 16:
                wait("sync", "sT", v)
                st["w2_load"][(ti, j + 3)] = (WSEM[(j + 3) % 3], op(
                    "sync", lambda e, j=j: e.dma_start(
                        t[f"wf{(j + 3) % 3}"][:, :],
                        dr["w2_bf"][:, (j + 3) * 4096:(j + 4) * 4096]),
                    inc=WSEM[(j + 3) % 3]))
        vb2 = op("tensor", lambda e, pm=pm, mb=mb: e.matmul(
            pm[:, 0:256], t["w_b2"][:, mb * 128:(mb + 1) * 128],
            t["mask_b"][:, :], start=False, stop=True), inc="sT")
        wait("vector", "sT", vb2)
        if ti > 0 and mb == 0:
            wait("vector", "sS", st[("ln2sq", ti - 1)])
        ev = op("vector", lambda e, pm=pm, mb=mb: e.scalar_tensor_tensor(
            t["s2_bf"][:, mb * 256:(mb + 1) * 256], pm[:, 0:256], 0.0,
            t["y_bf"][:, mb * 256:(mb + 1) * 256], ALU.max, ALU.add), inc="sV")
        P.set_free(pn, "sV", ev)
        yield
    st[("s2_done", ti)] = ev

    wait("scalar", "sV", ev)
    vsq = op("scalar", lambda e: e.activation(t["qc"][:, :], t["s2_bf"][:, :],
                                              AF.Square), inc="sS")
    st[("ln2sq", ti)] = vsq
    P.wait_free("tensor", "pMS")
    wait("tensor", "sV", ev)
    for cb in range(8):
        op("tensor", lambda e, cb=cb: e.matmul(
            pMS[0:1, 0:256], t["c_ones128"][:, 0:1],
            t["s2_bf"][:, cb * 256:(cb + 1) * 256],
            start=(cb == 0), stop=(cb == 7)))
    wait("tensor", "sS", vsq)
    for cb in range(8):
        v = op("tensor", lambda e, cb=cb: e.matmul(
            pMS[0:1, 256:512], t["c_ones128"][:, 0:1],
            t["qc"][:, cb * 256:(cb + 1) * 256],
            start=(cb == 0), stop=(cb == 7)), inc=("sT" if cb == 7 else None))
    st[("ln2stats", ti)] = v
    yield
    emit_ln_tail(P, t, ps, st, ti, v, "ln2")
    wait("tensor", "sV", st[("ln2", ti)])
    P.wait_free("tensor", "pMS")
    vmu = op("tensor", lambda e: e.matmul(
        pMS[:, 0:256], t["c_bcast1"][:, :], t["frb"][0:1, 0:256],
        start=True, stop=True))
    vin = op("tensor", lambda e: e.matmul(
        pMS[:, 256:512], t["c_bcast1"][:, :], t["fqb"][0:1, 0:256],
        start=True, stop=True), inc="sT")
    wait("vector", "sT", vin)
    for hf in (0, 1):
        prev = st.get(("out_dma", ti, hf - 1)) if hf == 1 else (
            st.get(("out_dma", ti - 1, 1)) if ti > 0 else None)
        if prev is not None:
            wait("vector", "dS", prev)
        op("vector", lambda e, hf=hf: e.tensor_sub(
            t["out_f"][:, :].rearrange("p (c a) -> p c a", a=256),
            t["s2_bf"][:, hf * 1024:(hf + 1) * 1024].rearrange(
                "p (c a) -> p c a", a=256),
            bass.AP(pMS, 0, [[512, 128], [0, 4], [1, 256]])))
        vout = op("vector", lambda e, hf=hf: e.tensor_mul(
            t["out_f"][:, :].rearrange("p (c a) -> p c a", a=256),
            t["out_f"][:, :].rearrange("p (c a) -> p c a", a=256),
            bass.AP(pMS, 256, [[512, 128], [0, 4], [1, 256]])),
            inc="sV")
        wait("scalar", "sV", vout)
        st[("out_dma", ti, hf)] = op("scalar", lambda e, ti=ti, hf=hf: e.dma_start(
            dr["out_img"][:, ti * 2048 + hf * 1024:ti * 2048 + (hf + 1) * 1024],
            t["out_f"][:, :]), inc="dS")
        if hf == 1:
            P.set_free("pMS", "sV", vout)
        yield



def kernel(**inputs):
    imgs = _host_images(inputs)
    consts = _host_consts()
    x = np.asarray(inputs["L0_attn_x"], np.float32)
    nc = build_nc()
    in_maps = []
    for b in range(B):
        m = {}
        m.update(imgs)
        m.update({"d" + k: v for k, v in consts.items()})
        m["x_img"] = np.ascontiguousarray(
            x[b].reshape(NT, TILE, 8, 128).transpose(3, 0, 2, 1).reshape(128, 32768))
        in_maps.append(m)
    res = run_bass_kernel_spmd(nc, in_maps, core_ids=list(range(8)))
    outs = []
    for b in range(B):
        oi = np.asarray(res.results[b]["out_img"])
        outs.append(oi.reshape(128, NT, 8, TILE).transpose(1, 3, 2, 0).reshape(L, D))
    return np.stack(outs).astype(np.float32)
